# version 02
# baseline (speedup 1.0000x reference)
import contextlib
import numpy as np
import concourse.bass as bass
import concourse.mybir as mybir
from concourse.bass_utils import run_bass_kernel_spmd

F32 = mybir.dt.float32
BF16 = mybir.dt.bfloat16
AF = mybir.ActivationFunctionType
ALU = mybir.AluOpType
AX = mybir.AxisListType

D = 1024
DI = 2048
DN = 16
DTR = 64
DFF = 2816
TLOC = 4096
TT = 512
NT = TLOC // TT
SEQ = 8192
NDT = DI // 128
NDH = NDT // 2
NCH = SEQ // TT
NKD = D // 128
NKI = DI // 128
NFF = DFF // 128
NST = TLOC // 128

_DBG = False
QMIDS = (0.25, 0.75, 1.25, 1.75, 2.5, 3.5, 5.0)
QWS = (0.5, 0.5, 0.5, 0.5, 1.0, 1.0, 2.0)


class Sem:
    __slots__ = ("h", "name", "n")

    def __init__(self, h, name):
        self.h, self.name, self.n = h, name, 0


class Plan:
    ENGS = ("sync", "tensor", "scalar", "vector", "gpsimd")

    def __init__(self, nc):
        self.nc = nc
        self.q = {e: [] for e in self.ENGS}
        self.wm = {e: {} for e in self.ENGS}
        self.bank_tok = {}
        self.nblk = 0

    def op(self, eng, fn):
        self.q[eng].append(fn)

    def wait(self, eng, sem, val=None):
        v = sem.n if val is None else val
        if v <= 0:
            return
        if self.wm[eng].get(sem.name, -1) >= v:
            return
        self.wm[eng][sem.name] = v
        self.q[eng].append(lambda E, s=sem, v=v: E.wait_ge(s.h, v))

    def tok(self, sem):
        return (sem, sem.n)

    def wait_tok(self, eng, tok):
        if tok is not None:
            self.wait(eng, tok[0], tok[1])

    def use_bank(self, k):
        self.wait_tok("tensor", self.bank_tok.get(k))

    def done_bank(self, k, sem):
        self.bank_tok[k] = (sem, sem.n)

    def flush(self, name=None):
        qs = {e: self.q[e] for e in self.ENGS}
        self.q = {e: [] for e in self.ENGS}
        name = name or f"b{self.nblk}"
        self.nblk += 1
        if not any(qs.values()):
            return
        with self.nc.Block(name) as blk:
            for e in self.ENGS:
                fns = qs[e]
                if fns:
                    getattr(blk, e)(lambda E, fns=fns: [f(E) for f in fns])

    def barrier(self, sem):
        for e in self.ENGS:
            self.q[e].append(lambda E, s=sem: E.sem_inc(s.h, 1))
            sem.n += 1
        for e in self.ENGS:
            self.wait(e, sem)


def build_program():
    nc = bass.Bass("TRN2", target_bir_lowering=False)

    d_x = nc.dram_tensor("x_loc", [TLOC, D], F32, kind="ExternalInput").ap()
    d_xhalo = nc.dram_tensor("x_halo", [D, 3], F32, kind="ExternalInput").ap()
    d_Win = nc.dram_tensor("W_in", [2 * DI, D], F32, kind="ExternalInput").ap()
    d_cw = nc.dram_tensor("conv_w", [DI, 4], F32, kind="ExternalInput").ap()
    d_cb = nc.dram_tensor("conv_b", [1, DI], F32, kind="ExternalInput").ap()
    d_Alog = nc.dram_tensor("A_log_dh", [DI // 2, DN], F32, kind="ExternalInput").ap()
    d_Wdt = nc.dram_tensor("W_dt", [DI, DTR], F32, kind="ExternalInput").ap()
    d_bdt = nc.dram_tensor("b_dt", [1, DI], F32, kind="ExternalInput").ap()
    d_WB = nc.dram_tensor("W_B", [DN, D], F32, kind="ExternalInput").ap()
    d_WC = nc.dram_tensor("W_C", [DN, D], F32, kind="ExternalInput").ap()
    d_Wout = nc.dram_tensor("W_out", [D, DI], F32, kind="ExternalInput").ap()
    d_wg = nc.dram_tensor("w_gate", [DFF, D], F32, kind="ExternalInput").ap()
    d_wu = nc.dram_tensor("w_up", [DFF, D], F32, kind="ExternalInput").ap()
    d_wd = nc.dram_tensor("w_down", [D, DFF], F32, kind="ExternalInput").ap()
    d_ident = nc.dram_tensor("ident", [128, 128], F32, kind="ExternalInput").ap()
    d_cst = nc.dram_tensor("csts", [128, 2], F32, kind="ExternalInput").ap()

    d_out = nc.dram_tensor("out_loc", [TLOC, D], F32, kind="ExternalOutput").ap()

    d_dt = nc.dram_tensor("dt_hbm", [NT, 128, NDT, TT], BF16, kind="Internal").ap()
    d_dx = nc.dram_tensor("dx_hbm", [NT, 128, NDT, TT], BF16, kind="Internal").ap()
    d_bc = nc.dram_tensor("bc_hbm", [NT, 2, DN, TT], BF16, kind="Internal").ap()
    d_zs = nc.dram_tensor("zs_hbm", [NT, 128, NDT, TT], BF16, kind="Internal").ap()
    d_dt_ag = nc.dram_tensor("dt_ag", [2, NT, 128, NDT, TT], BF16, kind="Internal").ap()
    d_dx_ag = nc.dram_tensor("dx_ag", [2, NT, 128, NDT, TT], BF16, kind="Internal").ap()
    d_bc_ag = nc.dram_tensor("bc_ag", [2, NT, 2, DN, TT], BF16, kind="Internal").ap()
    d_dt_my = nc.dram_tensor("dt_my", [NCH, 128, NDH, TT], BF16, kind="Internal").ap()
    d_dx_my = nc.dram_tensor("dx_my", [NCH, 128, NDH, TT], BF16, kind="Internal").ap()
    d_y = nc.dram_tensor("y_hbm", [NDH, 128, SEQ], BF16, kind="Internal").ap()
    d_y_ag = nc.dram_tensor("y_ag", [2, NDH, 128, SEQ], BF16, kind="Internal").ap()
    d_y_my = nc.dram_tensor("y_my", [2, NDH, 128, TLOC], BF16, kind="Internal").ap()
    d_x1 = nc.dram_tensor("x1_hbm", [NST, 128, D], F32, kind="Internal").ap()
    if _DBG:
        d_dbg = {k: nc.dram_tensor(f"dbg_{k}", shp, dt, kind="ExternalOutput").ap()
                 for k, shp, dt in (
                     ("dt", [NT, 128, NDT, TT], BF16),
                     ("dx", [NT, 128, NDT, TT], BF16),
                     ("zs", [NT, 128, NDT, TT], BF16),
                     ("bc", [NT, 2, DN, TT], BF16),
                     ("y", [NDH, 128, SEQ], BF16),
                     ("x1", [NST, 128, D], F32),
                     ("wgT", [128, NKD, DFF], BF16),
                     ("hst", [4, 128, NDH, DN], F32),
                     ("hall", [128, DN, TT], BF16),
                     ("brep", [2, 128, DN, TT], BF16),
                     ("crep", [2, 128, DN, TT], BF16),
                     ("bc2", [2, 33, DN, TT], BF16),
                     ("ymy", [2, NDH, 128, TLOC], BF16),
                     ("ao", [NT, 128, NKD, TT], F32),
                     ("ysz", [NT, 128, NKI, TT], BF16),
                     ("xr", [NST, 128, D], F32),
                     ("lns", [NST, 128, 8], F32),
                 )}
    d_wgT = nc.dram_tensor("wgT_hbm", [128, NKD, DFF], BF16, kind="Internal").ap()
    d_wuT = nc.dram_tensor("wuT_hbm", [128, NKD, DFF], BF16, kind="Internal").ap()
    d_wdT = nc.dram_tensor("wdT_hbm", [128, NFF, D], BF16, kind="Internal").ap()

    p = Plan(nc)

    def SB(stack):
        def sb(name, shape, dtype):
            return stack.enter_context(nc.sbuf_tensor(name, shape, dtype))
        return sb

    def mk_quant(sb, pfx, Fmax, s_ve, s_ac):
        q_xn = sb(f"{pfx}_xn", [128, Fmax], F32)
        q_bm = sb(f"{pfx}_bm", [128, Fmax // 16], F32)
        q_sc = sb(f"{pfx}_sc", [128, Fmax // 16], F32)
        q_ma = sb(f"{pfx}_ma", [128, Fmax], BF16)
        q_mb = sb(f"{pfx}_mb", [128, Fmax], BF16)
        q_a1 = sb(f"{pfx}_a1", [128, Fmax], BF16)

        def quant(src_ap, F, out_ap):
            nb = F // 16
            p.op("vector", lambda E: E.reduce_max(
                out=q_bm[:, 0:nb], in_=src_ap.rearrange("p (g k) -> p g k", k=16),
                axis=AX.X, apply_absolute_value=True).then_inc(s_ve.h, 1)); s_ve.n += 1
            p.op("vector", lambda E: E.tensor_scalar(
                out=q_sc[:, 0:nb], in0=q_bm[:, 0:nb], scalar1=1.0 / 6.0, scalar2=1e-12,
                op0=ALU.mult, op1=ALU.max).then_inc(s_ve.h, 1)); s_ve.n += 1
            p.op("vector", lambda E: E.reciprocal(
                out=q_bm[:, 0:nb], in_=q_sc[:, 0:nb]).then_inc(s_ve.h, 1)); s_ve.n += 1
            p.wait("scalar", s_ve)
            p.op("scalar", lambda E: E.activation(
                out=q_xn[:, 0:F], in_=src_ap, func=AF.Abs).then_inc(s_ac.h, 1)); s_ac.n += 1
            p.wait("vector", s_ac)
            p.op("vector", lambda E: E.tensor_tensor(
                out=q_xn[:, 0:F].rearrange("p (g k) -> p g k", k=16),
                in0=q_xn[:, 0:F].rearrange("p (g k) -> p g k", k=16),
                in1=q_bm[:, 0:nb].unsqueeze(2).broadcast_to([128, nb, 16]),
                op=ALU.mult).then_inc(s_ve.h, 1)); s_ve.n += 1

            def msk(dst, mid, w):
                p.op("vector", lambda E, d=dst, mm=mid, ww=w: E.tensor_scalar(
                    out=d, in0=q_xn[:, 0:F], scalar1=mm, scalar2=ww,
                    op0=ALU.is_gt, op1=ALU.mult).then_inc(s_ve.h, 1)); s_ve.n += 1

            def tt(dst, a, b, opx=ALU.add):
                p.op("vector", lambda E, d=dst, aa=a, bb=b, oo=opx: E.tensor_tensor(
                    out=d, in0=aa, in1=bb, op=oo).then_inc(s_ve.h, 1)); s_ve.n += 1

            msk(q_a1[:, 0:F], QMIDS[0], QWS[0])
            msk(q_mb[:, 0:F], QMIDS[1], QWS[1])
            tt(q_a1[:, 0:F], q_a1[:, 0:F], q_mb[:, 0:F])
            for k in (2, 4, 6):
                msk(q_ma[:, 0:F], QMIDS[k], QWS[k])
                if k + 1 < 7:
                    msk(q_mb[:, 0:F], QMIDS[k + 1], QWS[k + 1])
                    tt(q_ma[:, 0:F], q_ma[:, 0:F], q_mb[:, 0:F])
                tt(q_a1[:, 0:F], q_a1[:, 0:F], q_ma[:, 0:F])
            p.op("vector", lambda E: E.tensor_scalar(
                out=q_ma[:, 0:F], in0=src_ap, scalar1=0.0, scalar2=2.0,
                op0=ALU.is_gt, op1=ALU.mult).then_inc(s_ve.h, 1)); s_ve.n += 1
            p.op("vector", lambda E: E.scalar_tensor_tensor(
                out=q_mb[:, 0:F].rearrange("p (g k) -> p g k", k=16),
                in0=q_ma[:, 0:F].rearrange("p (g k) -> p g k", k=16),
                scalar=-1.0,
                in1=q_sc[:, 0:nb].unsqueeze(2).broadcast_to([128, nb, 16]),
                op0=ALU.add, op1=ALU.mult).then_inc(s_ve.h, 1)); s_ve.n += 1
            tt(out_ap, q_a1[:, 0:F], q_mb[:, 0:F], ALU.mult)

        return quant

    es = contextlib.ExitStack()
    with es:
        sbg = SB(es)
        s_ld = Sem(es.enter_context(nc.semaphore("s_ld")), "s_ld")
        s_st = Sem(es.enter_context(nc.semaphore("s_st")), "s_st")
        s_pe = Sem(es.enter_context(nc.semaphore("s_pe")), "s_pe")
        s_ve = Sem(es.enter_context(nc.semaphore("s_ve")), "s_ve")
        s_ac = Sem(es.enter_context(nc.semaphore("s_ac")), "s_ac")
        s_cc = Sem(es.enter_context(nc.semaphore("s_cc")), "s_cc")
        s_gp = Sem(es.enter_context(nc.semaphore("s_gp")), "s_gp")
        s_bc = Sem(es.enter_context(nc.semaphore("s_bc")), "s_bc")
        s_ph = Sem(es.enter_context(nc.semaphore("s_ph")), "s_ph")

        def dma(eng, out, in_, s, slow=False):
            if slow:
                p.op(eng, lambda E, o=out, i=in_, ss=s: E.dma_start(
                    out=o, in_=i, allow_slow_non_contiguous=True).then_inc(ss.h, 16))
            else:
                p.op(eng, lambda E, o=out, i=in_, ss=s: E.dma_start(
                    out=o, in_=i).then_inc(ss.h, 16))
            s.n += 16

        ident_f = sbg("ident_f", [128, 128], F32)
        ident_bf = sbg("ident_bf", [128, 128], BF16)
        ones_bf = sbg("ones_bf", [33, 128], BF16)
        cw = sbg("cw", [128, NDT, 4], F32)
        cb = sbg("cb", [128, NDT], F32)
        bdt = sbg("bdt", [128, NDT], F32)
        A_sb = sbg("A_sb", [128, NDH, DN], F32)
        ln_s = sbg("ln_s", [128, 8], F32)
        cst_sb = sbg("cst_sb", [128, 2], F32)
        pbank = [es.enter_context(nc.psum_tensor(f"pb{i}", [128, 512], F32)) for i in range(6)]
        tbank = [es.enter_context(nc.psum_tensor(f"tb{i}", [128, 1024], BF16)) for i in range(2)]

        dma("sync", cst_sb[:], d_cst[:, :], s_ld)
        nc.const_aps.register(F32, 1e-5, cst_sb[:, 0:1])
        nc.const_aps.register(F32, -0.5, cst_sb[:, 1:2])
        dma("sync", ident_f[:], d_ident[:, :], s_ld)
        dma("sync", cw[:], d_cw.rearrange("(dt p) k -> p dt k", p=128), s_ld)
        dma("sync", cb[:], d_cb.rearrange("o (dt p) -> (o p) dt", p=128), s_ld, slow=True)
        dma("sync", bdt[:], d_bdt.rearrange("o (dt p) -> (o p) dt", p=128), s_ld, slow=True)
        dma("sync", A_sb[:], d_Alog.rearrange("(dt p) n -> p dt n", p=128), s_ld)
        p.wait("vector", s_ld)
        p.op("vector", lambda E: E.tensor_copy(out=ident_bf[:], in_=ident_f[:]).then_inc(s_ve.h, 1)); s_ve.n += 1
        p.op("vector", lambda E: E.memset(ones_bf[:], 1.0).then_inc(s_ve.h, 1)); s_ve.n += 1
        p.wait("scalar", s_ld)
        p.op("scalar", lambda E: E.activation(out=A_sb[:], in_=A_sb[:], func=AF.Exp).then_inc(s_ac.h, 1)); s_ac.n += 1
        p.wait("vector", s_ac)
        p.op("vector", lambda E: E.tensor_scalar(out=A_sb[:], in0=A_sb[:], scalar1=-1.0,
                                                 scalar2=None, op0=ALU.mult).then_inc(s_ve.h, 1)); s_ve.n += 1

        es_w = contextlib.ExitStack()
        sbw = SB(es_w)
        WtOut = sbw("WtOut", [128, NKI, D], BF16)

        es_a = contextlib.ExitStack()
        sba = SB(es_a)
        WtIn = sba("WtIn", [128, NKD, 2 * DI], BF16)
        WtDt = sba("WtDt", [64, DI], BF16)
        WtB = sba("WtB", [128, NKD, DN], BF16)
        WtC = sba("WtC", [128, NKD, DN], BF16)

        es_0 = contextlib.ExitStack()
        sb0 = SB(es_0)
        wstg_f = sb0("wstg_f", [128, DFF], F32)
        wstg_bf = sb0("wstg_bf", [128, DFF], BF16)
        wT_out = sb0("wT_out", [128, NFF, 128], BF16)
        quant0 = mk_quant(sb0, "q0", DFF, s_ve, s_ac)

        def prep_wT(src_dram, M, K, put, quant):
            nmt = M // 128
            nkt = (K + 127) // 128
            for mt in range(nmt):
                p.wait("sync", s_ve)
                p.wait("sync", s_pe)
                p.wait("vector", s_st)
                dma("sync", wstg_f[:, 0:K], src_dram[mt * 128:(mt + 1) * 128, :], s_ld)
                p.wait("vector", s_ld)
                p.wait("scalar", s_ld)
                if quant:
                    quant0(wstg_f[:, 0:K], K, wstg_bf[:, 0:K])
                else:
                    p.op("vector", lambda E: E.tensor_copy(
                        out=wstg_bf[:, 0:K], in_=wstg_f[:, 0:K]).then_inc(s_ve.h, 1)); s_ve.n += 1
                p.wait("tensor", s_ve)
                for g in range((nkt + 7) // 8):
                    bk = "t%d" % (g % 2)
                    p.use_bank(bk)
                    kts = list(range(g * 8, min(g * 8 + 8, nkt)))
                    for kk, kt in enumerate(kts):
                        kw = min(128, K - kt * 128)
                        p.op("tensor", lambda E, b=tbank[g % 2], kk=kk, kt=kt, kw=kw: E.transpose(
                            b[0:kw, kk * 128:kk * 128 + 128],
                            wstg_bf[:, kt * 128:kt * 128 + kw],
                            ident_bf[:]).then_inc(s_pe.h, 1)); s_pe.n += 1
                    p.wait("vector", s_pe)
                    for kk, kt in enumerate(kts):
                        kw = min(128, K - kt * 128)
                        p.op("vector", lambda E, b=tbank[g % 2], kk=kk, kt=kt, kw=kw: E.tensor_copy(
                            out=wT_out[0:kw, kt, :],
                            in_=b[0:kw, kk * 128:kk * 128 + 128]).then_inc(s_ve.h, 1)); s_ve.n += 1
                    p.done_bank(bk, s_ve)
                put(mt)

        def put_win(mt):
            for kt in range(NKD):
                p.op("vector", lambda E, kt=kt, mt=mt: E.tensor_copy(
                    out=WtIn[:, kt, mt * 128:(mt + 1) * 128],
                    in_=wT_out[:, kt, :]).then_inc(s_ve.h, 1)); s_ve.n += 1
        prep_wT(d_Win, 2 * DI, D, put_win, quant=False)

        def put_wdt(mt):
            p.op("vector", lambda E, mt=mt: E.tensor_copy(
                out=WtDt[:, mt * 128:(mt + 1) * 128],
                in_=wT_out[0:64, 0, :]).then_inc(s_ve.h, 1)); s_ve.n += 1
        prep_wT(d_Wdt, DI, DTR, put_wdt, quant=False)

        def prep_bc(src_dram, dst):
            p.wait("sync", s_ve)
            p.wait("sync", s_pe)
            dma("sync", wstg_f[0:DN, 0:D], src_dram[:, :], s_ld)
            p.wait("vector", s_ld)
            p.op("vector", lambda E: E.tensor_copy(
                out=wstg_bf[0:DN, 0:D], in_=wstg_f[0:DN, 0:D]).then_inc(s_ve.h, 1)); s_ve.n += 1
            p.wait("tensor", s_ve)
            for kt in range(NKD):
                bk = "t%d" % (kt % 2)
                p.use_bank(bk)
                p.op("tensor", lambda E, b=tbank[kt % 2], kt=kt: E.transpose(
                    b[:, 0:DN], wstg_bf[0:DN, kt * 128:(kt + 1) * 128],
                    ident_bf[0:DN, 0:DN]).then_inc(s_pe.h, 1)); s_pe.n += 1
                p.wait("vector", s_pe)
                p.op("vector", lambda E, b=tbank[kt % 2], kt=kt: E.tensor_copy(
                    out=dst[:, kt, :], in_=b[:, 0:DN]).then_inc(s_ve.h, 1)); s_ve.n += 1
                p.done_bank(bk, s_ve)
        prep_bc(d_WB, WtB)
        prep_bc(d_WC, WtC)

        def put_wout(mt):
            for kt in range(NKI):
                p.op("vector", lambda E, kt=kt, mt=mt: E.tensor_copy(
                    out=WtOut[:, kt, mt * 128:(mt + 1) * 128],
                    in_=wT_out[:, kt, :]).then_inc(s_ve.h, 1)); s_ve.n += 1
        prep_wT(d_Wout, D, DI, put_wout, quant=False)

        def put_ffn(dst_hbm, nkt):
            def f(mt):
                p.wait("gpsimd", s_ve)
                dma("gpsimd", dst_hbm[:, :, mt * 128:(mt + 1) * 128],
                    wT_out[:, 0:nkt, :], s_st)
            return f
        prep_wT(d_wg, DFF, D, put_ffn(d_wgT, NKD), quant=True)
        prep_wT(d_wu, DFF, D, put_ffn(d_wuT, NKD), quant=True)
        prep_wT(d_wd, D, DFF, put_ffn(d_wdT, NFF), quant=True)

        p.barrier(s_ph)
        p.flush("p0")
        es_0.close()

        es_1 = contextlib.ExitStack()
        sb1 = SB(es_1)
        xt_f = sb1("xt_f", [128, 2, D], F32)
        xt_bf = sb1("xt_bf", [128, 2, D], BF16)
        x_fm = sb1("x_fm", [128, NKD, TT], BF16)
        xval = sb1("xval", [128, NDT, TT + 3], BF16)
        zsil = sb1("zsil", [128, NDT, TT], BF16)
        dt_bf = sb1("dt_bf", [128, NDT, TT], BF16)
        dxv = sb1("dxv", [128, NDT, TT], BF16)
        xvs = sb1("xvs", [128, NDT, TT], BF16)
        csc = sb1("csc", [128, TT], F32)
        bcs = sb1("bcs", [DN, 2, TT], BF16)
        xh_f = sb1("xh_f", [128, NKD, 3], F32)
        xh_bf = sb1("xh_bf", [128, NKD, 3], BF16)

        dma("sync", xh_f[:], d_xhalo.rearrange("(k p) t -> p k t", p=128), s_ld)
        p.wait("vector", s_ld)
        p.op("vector", lambda E: E.tensor_copy(out=xh_bf[:], in_=xh_f[:]).then_inc(s_ve.h, 1)); s_ve.n += 1

        for i in range(NT):
            for hf in range(2):
                p.wait("sync", s_ve)
                p.wait("vector", s_pe)
                dma("sync", xt_f[:], d_x[i * TT + hf * 256:i * TT + (hf + 1) * 256, :]
                    .rearrange("(s p) d -> p s d", p=128), s_ld)
                p.wait("vector", s_ld)
                p.op("vector", lambda E: E.tensor_copy(out=xt_bf[:], in_=xt_f[:]).then_inc(s_ve.h, 1)); s_ve.n += 1
                p.wait("tensor", s_ve)
                for ft in range(NKD):
                    bk = "t%d" % (ft % 2)
                    p.use_bank(bk)
                    for st in range(2):
                        p.op("tensor", lambda E, b=tbank[ft % 2], st=st, ft=ft: E.transpose(
                            b[:, st * 128:(st + 1) * 128],
                            xt_bf[:, st, ft * 128:(ft + 1) * 128],
                            ident_bf[:]).then_inc(s_pe.h, 1)); s_pe.n += 1
                    p.wait("vector", s_pe)
                    p.op("vector", lambda E, b=tbank[ft % 2], ft=ft, hf=hf: E.tensor_copy(
                        out=x_fm[:, ft, hf * 256:(hf + 1) * 256], in_=b[:, 0:256]).then_inc(s_ve.h, 1)); s_ve.n += 1
                    p.done_bank(bk, s_ve)

            if i == 0:
                p.wait("tensor", s_ve)
                for dt_i in range(NDT):
                    bk = 4 + (dt_i % 2)
                    p.use_bank(bk)
                    def hmm(E, b=pbank[bk], dt_i=dt_i):
                        for kt in range(NKD):
                            ins = E.matmul(b[:, 0:3], WtIn[:, kt, dt_i * 128:(dt_i + 1) * 128],
                                           xh_bf[:, kt, :], start=(kt == 0), stop=(kt == NKD - 1))
                            if kt == NKD - 1:
                                ins.then_inc(s_pe.h, 1)
                    p.op("tensor", hmm); s_pe.n += 1
                    p.wait("vector", s_pe)
                    p.op("vector", lambda E, b=pbank[bk], dt_i=dt_i: E.tensor_copy(
                        out=xval[:, dt_i, 0:3], in_=b[:, 0:3]).then_inc(s_ve.h, 1)); s_ve.n += 1
                    p.done_bank(bk, s_ve)
            else:
                p.op("vector", lambda E: E.tensor_copy(
                    out=xval[:, :, 0:3], in_=xval[:, :, TT:TT + 3]).then_inc(s_ve.h, 1)); s_ve.n += 1

            for ft in range(2 * NKI):
                bk = ft % 2
                p.use_bank(bk)
                def mmxz(E, b=pbank[bk], ft=ft):
                    for kt in range(NKD):
                        ins = E.matmul(b[:], WtIn[:, kt, ft * 128:(ft + 1) * 128],
                                       x_fm[:, kt, :], start=(kt == 0), stop=(kt == NKD - 1))
                        if kt == NKD - 1:
                            ins.then_inc(s_pe.h, 1)
                p.op("tensor", mmxz); s_pe.n += 1
                if ft < NDT:
                    p.wait("vector", s_pe)
                    p.op("vector", lambda E, b=pbank[bk], ft=ft: E.tensor_copy(
                        out=xval[:, ft, 3:TT + 3], in_=b[:]).then_inc(s_ve.h, 1)); s_ve.n += 1
                    p.done_bank(bk, s_ve)
                else:
                    p.wait("scalar", s_pe)
                    p.op("scalar", lambda E, b=pbank[bk], ft=ft: E.activation(
                        out=zsil[:, ft - NDT, :], in_=b[:], func=AF.Silu).then_inc(s_ac.h, 1)); s_ac.n += 1
                    p.done_bank(bk, s_ac)

            for dt_i in range(NDT):
                bk = 2 + (dt_i % 2)
                p.use_bank(bk)
                p.op("tensor", lambda E, b=pbank[bk], dt_i=dt_i: E.matmul(
                    b[:], WtDt[:, dt_i * 128:(dt_i + 1) * 128], x_fm[0:64, 0, :],
                    start=True, stop=True).then_inc(s_pe.h, 1)); s_pe.n += 1
                p.wait("scalar", s_pe)
                p.op("scalar", lambda E, b=pbank[bk], dt_i=dt_i: E.activation(
                    out=b[:], in_=b[:], func=AF.Exp, bias=bdt[:, dt_i:dt_i + 1]).then_inc(s_ac.h, 1)); s_ac.n += 1
                p.op("scalar", lambda E, b=pbank[bk], dt_i=dt_i: E.activation(
                    out=dt_bf[:, dt_i, :], in_=b[:], func=AF.Ln, bias=1.0).then_inc(s_ac.h, 1)); s_ac.n += 1
                p.done_bank(bk, s_ac)

            for which, wt in ((0, WtB), (1, WtC)):
                bk = 4 + which
                p.use_bank(bk)
                def mmbc(E, b=pbank[bk], wt=wt):
                    for kt in range(NKD):
                        ins = E.matmul(b[0:DN, :], wt[:, kt, :], x_fm[:, kt, :],
                                       start=(kt == 0), stop=(kt == NKD - 1))
                        if kt == NKD - 1:
                            ins.then_inc(s_pe.h, 1)
                p.op("tensor", mmbc); s_pe.n += 1
                p.wait("vector", s_pe)
                p.op("vector", lambda E, b=pbank[bk], w=which: E.tensor_copy(
                    out=bcs[:, w, :], in_=b[0:DN, :]).then_inc(s_ve.h, 1)); s_ve.n += 1
                p.done_bank(bk, s_ve)

            for dt_i in range(NDT):
                p.op("vector", lambda E, d=dt_i: E.tensor_scalar(
                    out=csc[:], in0=xval[:, d, 0:TT], scalar1=cw[:, d, 0:1],
                    scalar2=None, op0=ALU.mult).then_inc(s_ve.h, 1)); s_ve.n += 1
                for k in (1, 2, 3):
                    p.op("vector", lambda E, d=dt_i, k=k: E.scalar_tensor_tensor(
                        out=csc[:], in0=xval[:, d, k:TT + k], scalar=cw[:, d, k:k + 1],
                        in1=csc[:], op0=ALU.mult, op1=ALU.add).then_inc(s_ve.h, 1)); s_ve.n += 1
                p.wait("scalar", s_ve)
                p.op("scalar", lambda E, d=dt_i: E.activation(
                    out=xvs[:, d, :], in_=csc[:], func=AF.Silu,
                    bias=cb[:, d:d + 1]).then_inc(s_ac.h, 1)); s_ac.n += 1
                p.wait("vector", s_ac)
                p.op("vector", lambda E, d=dt_i: E.tensor_tensor(
                    out=dxv[:, d, :], in0=dt_bf[:, d, :], in1=xvs[:, d, :],
                    op=ALU.mult).then_inc(s_ve.h, 1)); s_ve.n += 1

            p.wait("gpsimd", s_ve)
            p.wait("gpsimd", s_ac)
            dma("gpsimd", d_dt[i], dt_bf[:], s_st)
            dma("gpsimd", d_dx[i], dxv[:], s_st)
            dma("gpsimd", d_zs[i], zsil[:], s_st)
            dma("gpsimd", d_bc[i, 0], bcs[:, 0, :], s_st)
            dma("gpsimd", d_bc[i, 1], bcs[:, 1, :], s_st)

        rg = [[0, 1], [2, 3], [4, 5], [6, 7]]
        p.wait("gpsimd", s_st)
        if _DBG:
            dma("gpsimd", d_dbg["dt"][:], d_dt[:], s_st)
            dma("gpsimd", d_dbg["dx"][:], d_dx[:], s_st)
            dma("gpsimd", d_dbg["zs"][:], d_zs[:], s_st)
            dma("gpsimd", d_dbg["bc"][:], d_bc[:], s_st)
            dma("gpsimd", d_dbg["wgT"][:], d_wgT[:], s_st)
        for a, b in ((d_dt, d_dt_ag), (d_dx, d_dx_ag), (d_bc, d_bc_ag)):
            p.op("gpsimd", lambda E, a=a, b=b: E.collective_compute(
                "AllGather", ALU.bypass, replica_groups=rg,
                ins=[a[:]], outs=[b[:]]).then_inc(s_cc.h, 1)); s_cc.n += 1
        p.barrier(s_ph)
        p.flush("p1a")
        es_1.close()
        es_a.close()

        _ccv = s_cc.n
        for arm in nc.switch_core_id(engines=[nc.sync], n=128):
            h = (arm.logical % 8) & 1
            nc.sync.wait_ge(s_cc.h, _ccv)
            nc.sync.dma_start(
                out=d_dt_my[:],
                in_=d_dt_ag.rearrange("w t p d c -> (w t) p d c")[:, :, 8 * h:8 * h + 8, :]
            ).then_inc(s_ld.h, 16)
            nc.sync.dma_start(
                out=d_dx_my[:],
                in_=d_dx_ag.rearrange("w t p d c -> (w t) p d c")[:, :, 8 * h:8 * h + 8, :]
            ).then_inc(s_ld.h, 16)
        s_ld.n += 32
        p.wm["sync"][s_cc.name] = _ccv
        restage_tok = p.tok(s_ld)

        es_b = contextlib.ExitStack()
        sbb = SB(es_b)
        bc2 = sbb("bc2", [33, DN, TT], BF16)
        Brep = sbb("Brep", [128, DN, TT], BF16)
        Crep = sbb("Crep", [128, DN, TT], BF16)
        dts = sbb("dts", [128, 2, NDH, TT], BF16)
        dxs = sbb("dxs", [128, 2, NDH, TT], BF16)
        dA_f = sbb("dA_f", [128, 2, TT], F32)
        u_all = sbb("u_all", [128, DN, TT], BF16)
        h_all = sbb("h_all", [128, DN, TT], BF16)
        hc_all = sbb("hc_all", [128, DN, TT], BF16)
        hstate = sbb("hstate", [128, NDH, DN], F32)
        y_sb = sbb("y_sb", [128, 2, TT], BF16)
        ysb_tok = [None, None]

        p.wait("gpsimd", s_cc)
        p.wait("vector", s_cc)
        p.op("vector", lambda E: E.memset(hstate[:], 0.0).then_inc(s_ve.h, 1)); s_ve.n += 1

        slice_toks = {}
        bcl_toks = {}
        scan_hist = [None] * 16
        for c in range(NCH):
            t0 = c * TT
            hh, tt_i = divmod(c, NT)
            tokb = bcl_toks.get(c - 1)
            if tokb is not None:
                p.wait("gpsimd", tokb[0], tokb[1])
            dma("gpsimd", bc2[0:1].rearrange("o n t -> o (n t)"),
                d_bc_ag[hh, tt_i, 0].rearrange("n t -> () (n t)"), s_bc)
            dma("gpsimd", bc2[32:33].rearrange("o n t -> o (n t)"),
                d_bc_ag[hh, tt_i, 1].rearrange("n t -> () (n t)"), s_bc)
            bc_tok = p.tok(s_bc)
            p.wait("tensor", s_bc, bc_tok[1])
            for n in range(DN):
                bk = n % 3
                p.use_bank(bk)
                p.use_bank(3 + bk)
                def bcmm(E, b=pbank[bk], b2=pbank[3 + bk], n=n):
                    E.matmul(b[:], ones_bf[0:1, :], bc2[0:1, n, :],
                             start=True, stop=True)
                    E.matmul(b2[:], ones_bf[32:33, :], bc2[32:33, n, :],
                             start=True, stop=True).then_inc(s_pe.h, 1)
                p.op("tensor", bcmm); s_pe.n += 1
                p.wait("vector", s_pe)
                p.op("vector", lambda E, b=pbank[bk], n=n: E.tensor_copy(
                    out=Brep[:, n, :], in_=b[:]).then_inc(s_ve.h, 1)); s_ve.n += 1
                p.op("vector", lambda E, b2=pbank[3 + bk], n=n: E.tensor_copy(
                    out=Crep[:, n, :], in_=b2[:]).then_inc(s_ve.h, 1)); s_ve.n += 1
                p.done_bank(bk, s_ve)
                p.done_bank(3 + bk, s_ve)
            rep_tok = p.tok(s_ve)
            bcl_toks[c] = p.tok(s_pe)
            if _DBG and c < 2:
                p.wait("gpsimd", s_ve)
                dma("gpsimd", d_dbg["brep"][c], Brep[:], s_st)
                dma("gpsimd", d_dbg["crep"][c], Crep[:], s_st)
                dma("gpsimd", d_dbg["bc2"][c], bc2[:], s_st)
                bcl_toks[c] = (s_st, s_st.n)

            cb_i = c % 2
            p.wait_tok("sync", restage_tok)
            tokp = slice_toks.get(c - 2)
            if tokp is not None:
                p.wait("sync", tokp[0], tokp[1])
            dma("sync", dts[:, cb_i, :, :], d_dt_my[c], s_ld)
            dma("sync", dxs[:, cb_i, :, :], d_dx_my[c], s_ld)
            ld_tok = p.tok(s_ld)

            for j in range(NDH):
                p.wait_tok("vector", ld_tok)
                p.wait_tok("vector", rep_tok)
                p.op("vector", lambda E, cb_i=cb_i, j=j: E.tensor_tensor(
                    out=u_all[:],
                    in0=dxs[:, cb_i, j, :].unsqueeze(1).broadcast_to([128, DN, TT]),
                    in1=Brep[:], op=ALU.mult).then_inc(s_ve.h, 1)); s_ve.n += 1
                p.wait("vector", s_gp)
                for n in range(DN):
                    p.wait("scalar", s_ld, ld_tok[1])
                    if scan_hist[(n + 14) % 16] is not None:
                        p.wait("scalar", s_ve, scan_hist[(n + 14) % 16])
                    p.op("scalar", lambda E, cb_i=cb_i, j=j, n=n: E.activation(
                        out=dA_f[:, n % 2, :], in_=dts[:, cb_i, j, :], func=AF.Exp,
                        scale=A_sb[:, j, n:n + 1]).then_inc(s_ac.h, 1)); s_ac.n += 1
                    p.wait("vector", s_ac)
                    p.op("vector", lambda E, j=j, n=n: E.tensor_tensor_scan(
                        out=h_all[:, n, :], data0=dA_f[:, n % 2, :], data1=u_all[:, n, :],
                        initial=hstate[:, j, n:n + 1], op0=ALU.mult,
                        op1=ALU.add).then_inc(s_ve.h, 1)); s_ve.n += 1
                    scan_hist[n] = s_ve.n
                p.wait("gpsimd", s_ve)
                p.op("gpsimd", lambda E, j=j: E.tensor_copy(
                    out=hstate[:, j, :], in_=h_all[:, :, TT - 1]).then_inc(s_gp.h, 1)); s_gp.n += 1
                p.op("vector", lambda E: E.tensor_tensor(
                    out=hc_all[:], in0=h_all[:], in1=Crep[:], op=ALU.mult).then_inc(s_ve.h, 1)); s_ve.n += 1
                p.op("vector", lambda E: E.tensor_tensor(
                    out=hc_all[:, 0:8, :], in0=hc_all[:, 0:8, :], in1=hc_all[:, 8:16, :],
                    op=ALU.add).then_inc(s_ve.h, 1)); s_ve.n += 1
                p.op("vector", lambda E: E.tensor_tensor(
                    out=hc_all[:, 0:4, :], in0=hc_all[:, 0:4, :], in1=hc_all[:, 4:8, :],
                    op=ALU.add).then_inc(s_ve.h, 1)); s_ve.n += 1
                p.op("vector", lambda E: E.tensor_tensor(
                    out=hc_all[:, 0:2, :], in0=hc_all[:, 0:2, :], in1=hc_all[:, 2:4, :],
                    op=ALU.add).then_inc(s_ve.h, 1)); s_ve.n += 1
                jb2 = j % 2
                if ysb_tok[jb2] is not None:
                    p.wait("vector", ysb_tok[jb2][0], ysb_tok[jb2][1])
                p.op("vector", lambda E, jb2=jb2: E.tensor_tensor(
                    out=y_sb[:, jb2, :], in0=hc_all[:, 0, :], in1=hc_all[:, 1, :],
                    op=ALU.add).then_inc(s_ve.h, 1)); s_ve.n += 1
                p.wait("gpsimd", s_ve)
                dma("gpsimd", d_y[j, :, t0:t0 + TT], y_sb[:, jb2, :], s_st)
                ysb_tok[jb2] = p.tok(s_st)
            if _DBG and c < 4:
                p.wait("gpsimd", s_gp)
                dma("gpsimd", d_dbg["hst"][c], hstate[:], s_st)
                if c == 0:
                    dma("gpsimd", d_dbg["hall"][:], h_all[:], s_st)
            slice_toks[c] = (s_st, s_st.n)

        p.wait("gpsimd", s_st)
        p.op("gpsimd", lambda E: E.collective_compute(
            "AllGather", ALU.bypass, replica_groups=rg,
            ins=[d_y[:]], outs=[d_y_ag[:]]).then_inc(s_cc.h, 1)); s_cc.n += 1
        if _DBG:
            dma("gpsimd", d_dbg["y"][:], d_y[:], s_st)
        p.barrier(s_ph)
        p.flush("p1b")
        es_b.close()

        _ccv2 = s_cc.n
        for arm in nc.switch_core_id(engines=[nc.sync], n=128):
            h = (arm.logical % 8) & 1
            nc.sync.wait_ge(s_cc.h, _ccv2)
            nc.sync.dma_start(
                out=d_y_my[:],
                in_=d_y_ag[:, :, :, h * TLOC:(h + 1) * TLOC]).then_inc(s_ld.h, 16)
        s_ld.n += 16
        p.wm["sync"][s_cc.name] = _ccv2
        yres_tok = p.tok(s_ld)

        es_c = contextlib.ExitStack()
        sbc = SB(es_c)
        ysz = sbc("ysz", [128, NKI, TT], BF16)
        zs2 = sbc("zs2", [128, NKI, TT], BF16)
        ao_f = sbc("ao_f", [128, NKD, TT], F32)
        aot_f = sbc("aot_f", [128, D], F32)
        xr_f = sbc("xr_f", [128, D], F32)
        x1_f = sbc("x1_f", [128, 4, D], F32)
        sq_f = sbc("sq_f", [128, D], F32)

        def ln_norm(src_col_ap, ob_src, dst_fn):
            pass

        if _DBG:
            p.wait_tok("gpsimd", yres_tok)
            dma("gpsimd", d_dbg["ymy"][:], d_y_my[:], s_st)
        for i in range(NT):
            p.wait_tok("sync", yres_tok)
            p.wait("sync", s_ve)
            p.wait("vector", s_st)
            for jg in range(NDT):
                dma("sync", ysz[:, jg, :], d_y_my[jg // NDH, jg % NDH, :, i * TT:(i + 1) * TT], s_ld)
            dma("sync", zs2[:], d_zs[i], s_ld)
            p.wait("vector", s_ld)
            p.op("vector", lambda E: E.tensor_tensor(
                out=ysz[:], in0=ysz[:], in1=zs2[:], op=ALU.mult).then_inc(s_ve.h, 1)); s_ve.n += 1
            p.wait("tensor", s_ve)
            for ft in range(NKD):
                bk = ft % 2
                p.use_bank(bk)
                def mmo(E, b=pbank[bk], ft=ft):
                    for kt in range(NKI):
                        ins = E.matmul(b[:], WtOut[:, kt, ft * 128:(ft + 1) * 128],
                                       ysz[:, kt, :], start=(kt == 0), stop=(kt == NKI - 1))
                        if kt == NKI - 1:
                            ins.then_inc(s_pe.h, 1)
                p.op("tensor", mmo); s_pe.n += 1
                p.wait("vector", s_pe)
                p.op("vector", lambda E, b=pbank[bk], ft=ft: E.tensor_copy(
                    out=ao_f[:, ft, :], in_=b[:]).then_inc(s_ve.h, 1)); s_ve.n += 1
                p.done_bank(bk, s_ve)

            if _DBG:
                p.wait("gpsimd", s_ve)
                dma("gpsimd", d_dbg["ao"][i], ao_f[:], s_st)
                dma("gpsimd", d_dbg["ysz"][i], ysz[:], s_st)
            for st in range(4):
                p.wait("tensor", s_ve)
                p.use_bank(2)
                p.use_bank(3)
                def tx(E, st=st):
                    for ft in range(NKD):
                        b = pbank[2] if ft < 4 else pbank[3]
                        col = (ft % 4) * 128
                        ins = E.transpose(b[:, col:col + 128],
                                          ao_f[:, ft, st * 128:(st + 1) * 128], ident_f[:])
                        if ft == NKD - 1:
                            ins.then_inc(s_pe.h, 1)
                p.op("tensor", tx); s_pe.n += 1
                p.wait("sync", s_ve)
                dma("sync", xr_f[:], d_x[i * TT + st * 128:i * TT + (st + 1) * 128, :], s_ld)
                p.wait("vector", s_pe)
                p.wait("vector", s_ld)
                p.op("vector", lambda E: E.tensor_copy(
                    out=aot_f[:, 0:512], in_=pbank[2][:]).then_inc(s_ve.h, 1)); s_ve.n += 1
                p.op("vector", lambda E: E.tensor_copy(
                    out=aot_f[:, 512:1024], in_=pbank[3][:]).then_inc(s_ve.h, 1)); s_ve.n += 1
                p.done_bank(2, s_ve)
                p.done_bank(3, s_ve)
                p.op("vector", lambda E: E.tensor_tensor(
                    out=xr_f[:], in0=xr_f[:], in1=aot_f[:], op=ALU.add).then_inc(s_ve.h, 1)); s_ve.n += 1
                p.op("vector", lambda E: E.reduce_sum(
                    out=ln_s[:, 0:1], in_=xr_f[:], axis=AX.X).then_inc(s_ve.h, 1)); s_ve.n += 1
                p.wait("scalar", s_ve)
                p.op("scalar", lambda E: E.activation(
                    out=sq_f[:], in_=xr_f[:], func=AF.Square).then_inc(s_ac.h, 1)); s_ac.n += 1
                p.wait("vector", s_ac)
                p.op("vector", lambda E: E.reduce_sum(
                    out=ln_s[:, 1:2], in_=sq_f[:], axis=AX.X).then_inc(s_ve.h, 1)); s_ve.n += 1
                p.wait("gpsimd", s_ve)
                p.op("gpsimd", lambda E: E.tensor_scalar(
                    out=ln_s[:, 2:3], in0=ln_s[:, 0:1], scalar1=1.0 / D, scalar2=None,
                    op0=ALU.mult)); 
                p.op("gpsimd", lambda E: E.tensor_tensor(
                    out=ln_s[:, 3:4], in0=ln_s[:, 2:3], in1=ln_s[:, 2:3], op=ALU.mult))
                p.op("gpsimd", lambda E: E.tensor_scalar(
                    out=ln_s[:, 4:5], in0=ln_s[:, 1:2], scalar1=1.0 / D, scalar2=None,
                    op0=ALU.mult))
                p.op("gpsimd", lambda E: E.tensor_tensor(
                    out=ln_s[:, 5:6], in0=ln_s[:, 4:5], in1=ln_s[:, 3:4], op=ALU.subtract))
                p.op("gpsimd", lambda E: E.tensor_scalar(
                    out=ln_s[:, 5:6], in0=ln_s[:, 5:6], scalar1=1e-5, scalar2=None,
                    op0=ALU.add).then_inc(s_gp.h, 1)); s_gp.n += 1
                p.wait("scalar", s_gp)
                p.op("scalar", lambda E: E.activation(
                    out=ln_s[:, 6:7], in_=ln_s[:, 5:6], func=AF.Ln).then_inc(s_ac.h, 1)); s_ac.n += 1
                p.op("scalar", lambda E: E.mul(out=ln_s[:, 6:7], in_=ln_s[:, 6:7], mul=-0.5).then_inc(s_ac.h, 1)); s_ac.n += 1
                p.op("scalar", lambda E: E.activation(
                    out=ln_s[:, 7:8], in_=ln_s[:, 6:7], func=AF.Exp).then_inc(s_ac.h, 1)); s_ac.n += 1
                p.wait("vector", s_ac)
                p.op("vector", lambda E, st=st: E.tensor_scalar(
                    out=x1_f[:, st, :], in0=xr_f[:], scalar1=ln_s[:, 2:3],
                    scalar2=ln_s[:, 7:8], op0=ALU.subtract, op1=ALU.mult).then_inc(s_ve.h, 1)); s_ve.n += 1
                if _DBG:
                    p.wait("gpsimd", s_ve)
                    dma("gpsimd", d_dbg["xr"][i * 4 + st], xr_f[:], s_st)
                    dma("gpsimd", d_dbg["lns"][i * 4 + st], ln_s[:], s_st)
            p.wait("gpsimd", s_ve)
            for st in range(4):
                dma("gpsimd", d_x1[i * 4 + st], x1_f[:, st, :], s_st)
        if _DBG:
            p.wait("gpsimd", s_st)
            dma("gpsimd", d_dbg["x1"][:], d_x1[:], s_st)
        p.barrier(s_ph)
        p.flush("p1c")
        es_c.close()
        es_w.close()

        es_2 = contextlib.ExitStack()
        sb2 = SB(es_2)
        WtG = sb2("WtG", [128, NKD, DFF], BF16)
        WtU_s = sb2("WtU_s", [128, 2, NKD, 128], BF16)
        WtD_s = sb2("WtD_s", [128, 2, NFF, 128], BF16)
        x1t = sb2("x1t", [128, 2, D], F32)
        x1r = sb2("x1r", [128, 2, D], F32)
        xq_tm = sb2("xq_tm", [128, 4, D], BF16)
        xq_fm = sb2("xq_fm", [128, NKD, TT], BF16)
        sg_t = sb2("sg_t", [128, TT], BF16)
        h_fm = sb2("h_fm", [128, NFF, TT], BF16)
        h_tm = sb2("h_tm", [128, DFF], BF16)
        do_f = sb2("do_f", [128, NKD, TT], F32)
        o_f = sb2("o_f", [128, 2, D], F32)
        sq_f = sb2("sq_f2", [128, D], F32)
        of_tok = [None, None]
        quant2 = mk_quant(sb2, "q2", DFF, s_ve, s_ac)

        p.wait("sync", s_st)
        dma("sync", WtG[:], d_wgT[:], s_ld)
        wld_tok = p.tok(s_ld)

        for i in range(NT):
            for st in range(4):
                stb = st % 2
                p.wait("sync", s_ve)
                p.wait("sync", s_ac)
                dma("sync", x1t[:, stb, :], d_x1[i * 4 + st], s_ld)
                p.wait("vector", s_ld)
                p.wait("scalar", s_ld)
                quant2(x1t[:, stb, :], D, xq_tm[:, st, :])
            p.wait("tensor", s_ve)
            for ft in range(NKD):
                bk = "t%d" % (ft % 2)
                p.use_bank(bk)
                def txq(E, ft=ft):
                    for st in range(4):
                        ins = E.transpose(tbank[ft % 2][:, st * 128:(st + 1) * 128],
                                          xq_tm[:, st, ft * 128:(ft + 1) * 128], ident_bf[:])
                        if st == 3:
                            ins.then_inc(s_pe.h, 1)
                p.op("tensor", txq); s_pe.n += 1
                p.wait("vector", s_pe)
                p.op("vector", lambda E, b=ft % 2, ft=ft: E.tensor_copy(
                    out=xq_fm[:, ft, :], in_=tbank[b][:, 0:512]).then_inc(s_ve.h, 1)); s_ve.n += 1
                p.done_bank(bk, s_ve)
            xq_tok = p.tok(s_ve)

            p.wait_tok("tensor", xq_tok)
            p.wait_tok("tensor", wld_tok)
            for ft in range(NFF):
                bkg, bku = 2 + (ft % 2), 4 + (ft % 2)
                wub = ft % 2
                p.wait("sync", s_pe)
                dma("sync", WtU_s[:, wub, :, :], d_wuT[:, :, ft * 128:(ft + 1) * 128], s_ld)
                p.wait("tensor", s_ld)
                p.use_bank(bkg)
                p.use_bank(bku)
                def mmgu(E, ft=ft, bkg=bkg, bku=bku, wub=wub):
                    for kt in range(NKD):
                        E.matmul(pbank[bkg][:], WtG[:, kt, ft * 128:(ft + 1) * 128],
                                 xq_fm[:, kt, :], start=(kt == 0), stop=(kt == NKD - 1))
                    for kt in range(NKD):
                        ins = E.matmul(pbank[bku][:], WtU_s[:, wub, kt, :],
                                       xq_fm[:, kt, :], start=(kt == 0), stop=(kt == NKD - 1))
                        if kt == NKD - 1:
                            ins.then_inc(s_pe.h, 1)
                p.op("tensor", mmgu); s_pe.n += 1
                p.wait("scalar", s_pe)
                p.op("scalar", lambda E, bkg=bkg: E.activation(
                    out=sg_t[:], in_=pbank[bkg][:], func=AF.Silu).then_inc(s_ac.h, 1)); s_ac.n += 1
                p.done_bank(bkg, s_ac)
                p.wait("vector", s_ac)
                p.op("vector", lambda E, bku=bku, ft=ft: E.tensor_tensor(
                    out=h_fm[:, ft, :], in0=sg_t[:], in1=pbank[bku][:],
                    op=ALU.mult).then_inc(s_ve.h, 1)); s_ve.n += 1
                p.done_bank(bku, s_ve)
            h_tok = p.tok(s_ve)

            for st in range(4):
                p.wait_tok("tensor", h_tok)
                for g in range(3):
                    bk = "t%d" % (g % 2)
                    p.use_bank(bk)
                    kts = list(range(g * 8, min(g * 8 + 8, NFF)))
                    def txh(E, st=st, kts=kts, g=g):
                        for kk, kt in enumerate(kts):
                            ins = E.transpose(tbank[g % 2][:, kk * 128:kk * 128 + 128],
                                              h_fm[:, kt, st * 128:(st + 1) * 128], ident_bf[:])
                            if kk == len(kts) - 1:
                                ins.then_inc(s_pe.h, 1)
                    p.op("tensor", txh); s_pe.n += 1
                    p.wait("vector", s_pe)
                    for kk, kt in enumerate(kts):
                        p.op("vector", lambda E, g=g, kk=kk, kt=kt: E.tensor_copy(
                            out=h_tm[:, kt * 128:(kt + 1) * 128],
                            in_=tbank[g % 2][:, kk * 128:kk * 128 + 128]).then_inc(s_ve.h, 1)); s_ve.n += 1
                    p.done_bank(bk, s_ve)
                p.wait("scalar", s_ve)
                quant2(h_tm[:], DFF, h_tm[:])
                p.wait("tensor", s_ve)
                for g in range(3):
                    bk = "t%d" % (g % 2)
                    p.use_bank(bk)
                    kts = list(range(g * 8, min(g * 8 + 8, NFF)))
                    def txhq(E, kts=kts, g=g):
                        for kk, kt in enumerate(kts):
                            ins = E.transpose(tbank[g % 2][:, kk * 128:kk * 128 + 128],
                                              h_tm[:, kt * 128:(kt + 1) * 128], ident_bf[:])
                            if kk == len(kts) - 1:
                                ins.then_inc(s_pe.h, 1)
                    p.op("tensor", txhq); s_pe.n += 1
                    p.wait("vector", s_pe)
                    p.op("vector", lambda E, g=g, kts=kts, st=st: E.tensor_copy(
                        out=h_fm[:, kts[0]:kts[0] + len(kts), st * 128:(st + 1) * 128],
                        in_=tbank[g % 2][:, 0:len(kts) * 128].rearrange(
                            "p (k c) -> p k c", c=128)).then_inc(s_ve.h, 1)); s_ve.n += 1
                    p.done_bank(bk, s_ve)
            hqf_tok = p.tok(s_ve)

            p.wait_tok("tensor", hqf_tok)
            for ft in range(NKD):
                wb = ft % 2
                p.wait("sync", s_pe)
                dma("sync", WtD_s[:, wb, :, :], d_wdT[:, :, ft * 128:(ft + 1) * 128], s_ld)
                p.wait("tensor", s_ld)
                bk = 2 + (ft % 2)
                p.use_bank(bk)
                def mmd(E, ft=ft, bk=bk, wb=wb):
                    for kt in range(NFF):
                        ins = E.matmul(pbank[bk][:], WtD_s[:, wb, kt, :],
                                       h_fm[:, kt, :], start=(kt == 0), stop=(kt == NFF - 1))
                        if kt == NFF - 1:
                            ins.then_inc(s_pe.h, 1)
                p.op("tensor", mmd); s_pe.n += 1
                p.wait("vector", s_pe)
                p.op("vector", lambda E, bk=bk, ft=ft: E.tensor_copy(
                    out=do_f[:, ft, :], in_=pbank[bk][:]).then_inc(s_ve.h, 1)); s_ve.n += 1
                p.done_bank(bk, s_ve)

            for st in range(4):
                stb = st % 2
                p.wait("sync", s_ve)
                dma("sync", x1r[:, stb, :], d_x1[i * 4 + st], s_ld)
                p.wait("tensor", s_ve)
                p.use_bank(4)
                p.use_bank(5)
                def txd(E, st=st):
                    for ft in range(NKD):
                        b = pbank[4] if ft < 4 else pbank[5]
                        col = (ft % 4) * 128
                        ins = E.transpose(b[:, col:col + 128],
                                          do_f[:, ft, st * 128:(st + 1) * 128], ident_f[:])
                        if ft == NKD - 1:
                            ins.then_inc(s_pe.h, 1)
                p.op("tensor", txd); s_pe.n += 1
                p.wait("vector", s_pe)
                ob = st % 2
                if of_tok[ob] is not None:
                    p.wait("vector", of_tok[ob][0], of_tok[ob][1])
                p.op("vector", lambda E, ob=ob: E.tensor_copy(
                    out=o_f[:, ob, 0:512], in_=pbank[4][:]).then_inc(s_ve.h, 1)); s_ve.n += 1
                p.op("vector", lambda E, ob=ob: E.tensor_copy(
                    out=o_f[:, ob, 512:1024], in_=pbank[5][:]).then_inc(s_ve.h, 1)); s_ve.n += 1
                p.done_bank(4, s_ve)
                p.done_bank(5, s_ve)
                p.wait("vector", s_ld)
                p.op("vector", lambda E, stb=stb, ob=ob: E.tensor_tensor(
                    out=o_f[:, ob, :], in0=o_f[:, ob, :], in1=x1r[:, stb, :],
                    op=ALU.add).then_inc(s_ve.h, 1)); s_ve.n += 1
                p.op("vector", lambda E, ob=ob: E.reduce_sum(
                    out=ln_s[:, 0:1], in_=o_f[:, ob, :], axis=AX.X).then_inc(s_ve.h, 1)); s_ve.n += 1
                p.wait("scalar", s_ve)
                p.op("scalar", lambda E, ob=ob: E.activation(
                    out=sq_f[:], in_=o_f[:, ob, :], func=AF.Square).then_inc(s_ac.h, 1)); s_ac.n += 1
                p.wait("vector", s_ac)
                p.op("vector", lambda E: E.reduce_sum(
                    out=ln_s[:, 1:2], in_=sq_f[:], axis=AX.X).then_inc(s_ve.h, 1)); s_ve.n += 1
                p.wait("gpsimd", s_ve)
                p.op("gpsimd", lambda E: E.tensor_scalar(
                    out=ln_s[:, 2:3], in0=ln_s[:, 0:1], scalar1=1.0 / D, scalar2=None,
                    op0=ALU.mult)); 
                p.op("gpsimd", lambda E: E.tensor_tensor(
                    out=ln_s[:, 3:4], in0=ln_s[:, 2:3], in1=ln_s[:, 2:3], op=ALU.mult))
                p.op("gpsimd", lambda E: E.tensor_scalar(
                    out=ln_s[:, 4:5], in0=ln_s[:, 1:2], scalar1=1.0 / D, scalar2=None,
                    op0=ALU.mult))
                p.op("gpsimd", lambda E: E.tensor_tensor(
                    out=ln_s[:, 5:6], in0=ln_s[:, 4:5], in1=ln_s[:, 3:4], op=ALU.subtract))
                p.op("gpsimd", lambda E: E.tensor_scalar(
                    out=ln_s[:, 5:6], in0=ln_s[:, 5:6], scalar1=1e-5, scalar2=None,
                    op0=ALU.add).then_inc(s_gp.h, 1)); s_gp.n += 1
                p.wait("scalar", s_gp)
                p.op("scalar", lambda E: E.activation(
                    out=ln_s[:, 6:7], in_=ln_s[:, 5:6], func=AF.Ln).then_inc(s_ac.h, 1)); s_ac.n += 1
                p.op("scalar", lambda E: E.mul(out=ln_s[:, 6:7], in_=ln_s[:, 6:7], mul=-0.5).then_inc(s_ac.h, 1)); s_ac.n += 1
                p.op("scalar", lambda E: E.activation(
                    out=ln_s[:, 7:8], in_=ln_s[:, 6:7], func=AF.Exp).then_inc(s_ac.h, 1)); s_ac.n += 1
                p.wait("vector", s_ac)
                p.op("vector", lambda E, ob=ob: E.tensor_scalar(
                    out=o_f[:, ob, :], in0=o_f[:, ob, :], scalar1=ln_s[:, 2:3],
                    scalar2=ln_s[:, 7:8], op0=ALU.subtract, op1=ALU.mult).then_inc(s_ve.h, 1)); s_ve.n += 1
                p.wait("gpsimd", s_ve)
                dma("gpsimd", d_out[i * TT + st * 128:i * TT + (st + 1) * 128, :],
                    o_f[:, ob, :], s_st)
                of_tok[ob] = p.tok(s_st)

        p.wait("gpsimd", s_st)
        p.flush("p2")
        es_2.close()
    return nc


_NC_CACHE = []


def kernel(L0_attn_x, L0_attn_W_in, L0_attn_conv_w, L0_attn_conv_b, L0_attn_A_log,
           L0_attn_W_dt, L0_attn_b_dt, L0_attn_W_B, L0_attn_W_C, L0_attn_W_out,
           L0_ffn_w_gate, L0_ffn_w_up, L0_ffn_w_down):
    x = np.ascontiguousarray(np.asarray(L0_attn_x, dtype=np.float32))
    W_in = np.ascontiguousarray(L0_attn_W_in, dtype=np.float32)
    conv_w = np.ascontiguousarray(np.asarray(L0_attn_conv_w, dtype=np.float32).reshape(DI, 4))
    conv_b = np.ascontiguousarray(np.asarray(L0_attn_conv_b, dtype=np.float32).reshape(1, DI))
    A_log = np.ascontiguousarray(L0_attn_A_log, dtype=np.float32)
    W_dt = np.ascontiguousarray(L0_attn_W_dt, dtype=np.float32)
    b_dt = np.ascontiguousarray(np.asarray(L0_attn_b_dt, dtype=np.float32).reshape(1, DI))
    W_B = np.ascontiguousarray(L0_attn_W_B, dtype=np.float32)
    W_C = np.ascontiguousarray(L0_attn_W_C, dtype=np.float32)
    W_out = np.ascontiguousarray(L0_attn_W_out, dtype=np.float32)
    w_gate = np.ascontiguousarray(L0_ffn_w_gate, dtype=np.float32)
    w_up = np.ascontiguousarray(L0_ffn_w_up, dtype=np.float32)
    w_down = np.ascontiguousarray(L0_ffn_w_down, dtype=np.float32)
    ident = np.eye(128, dtype=np.float32)

    if not _NC_CACHE:
        _NC_CACHE.append(build_program())
    nc = _NC_CACHE[0]

    in_maps = []
    for r in range(8):
        s, h = r >> 1, r & 1
        halo = (np.zeros((D, 3), np.float32) if h == 0
                else np.ascontiguousarray(x[s, TLOC - 3:TLOC, :].T))
        in_maps.append({
            "x_loc": np.ascontiguousarray(x[s, h * TLOC:(h + 1) * TLOC, :]),
            "x_halo": halo,
            "W_in": W_in,
            "conv_w": conv_w,
            "conv_b": conv_b,
            "A_log_dh": np.ascontiguousarray(A_log[h * (DI // 2):(h + 1) * (DI // 2), :]),
            "W_dt": W_dt,
            "b_dt": b_dt,
            "W_B": W_B,
            "W_C": W_C,
            "W_out": W_out,
            "w_gate": w_gate,
            "w_up": w_up,
            "w_down": w_down,
            "ident": ident,
            "csts": np.tile(np.array([[1e-5, -0.5]], dtype=np.float32), (128, 1)),
        })
    res = run_bass_kernel_spmd(nc, in_maps, core_ids=list(range(8)))
    out = np.empty((4, SEQ, D), np.float32)
    for r in range(8):
        s, h = r >> 1, r & 1
        out[s, h * TLOC:(h + 1) * TLOC, :] = res.results[r]["out_loc"]
    return out


# version 07
# speedup vs baseline: 1.1018x; 1.1018x over previous
import contextlib
import numpy as np
import concourse.bass as bass
import concourse.mybir as mybir
from concourse.bass_utils import run_bass_kernel_spmd

F32 = mybir.dt.float32
BF16 = mybir.dt.bfloat16
AF = mybir.ActivationFunctionType
ALU = mybir.AluOpType
AX = mybir.AxisListType

D = 1024
DI = 2048
DN = 16
DTR = 64
DFF = 2816
TLOC = 4096
TT = 512
NT = TLOC // TT
SEQ = 8192
NDT = DI // 128
NDH = NDT // 2
NCH = SEQ // TT
NKD = D // 128
NKI = DI // 128
NFF = DFF // 128
NST = TLOC // 128

_DBG = False
QMIDS = (0.25, 0.75, 1.25, 1.75, 2.5, 3.5, 5.0)
QWS = (0.5, 0.5, 0.5, 0.5, 1.0, 1.0, 2.0)


class Sem:
    __slots__ = ("h", "name", "n")

    def __init__(self, h, name):
        self.h, self.name, self.n = h, name, 0


class Plan:
    ENGS = ("sync", "tensor", "scalar", "vector", "gpsimd")

    def __init__(self, nc):
        self.nc = nc
        self.q = {e: [] for e in self.ENGS}
        self.wm = {e: {} for e in self.ENGS}
        self.bank_tok = {}
        self.nblk = 0

    def op(self, eng, fn):
        self.q[eng].append(fn)

    def wait(self, eng, sem, val=None):
        v = sem.n if val is None else val
        if v <= 0:
            return
        if self.wm[eng].get(sem.name, -1) >= v:
            return
        self.wm[eng][sem.name] = v
        self.q[eng].append(lambda E, s=sem, v=v: E.wait_ge(s.h, v))

    def tok(self, sem):
        return (sem, sem.n)

    def wait_tok(self, eng, tok):
        if tok is not None:
            self.wait(eng, tok[0], tok[1])

    def use_bank(self, k):
        self.wait_tok("tensor", self.bank_tok.get(k))

    def done_bank(self, k, sem):
        self.bank_tok[k] = (sem, sem.n)

    def flush(self, name=None):
        qs = {e: self.q[e] for e in self.ENGS}
        self.q = {e: [] for e in self.ENGS}
        name = name or f"b{self.nblk}"
        self.nblk += 1
        if not any(qs.values()):
            return
        with self.nc.Block(name) as blk:
            for e in self.ENGS:
                fns = qs[e]
                if fns:
                    getattr(blk, e)(lambda E, fns=fns: [f(E) for f in fns])

    def barrier(self, sem):
        for e in self.ENGS:
            self.q[e].append(lambda E, s=sem: E.sem_inc(s.h, 1))
            sem.n += 1
        for e in self.ENGS:
            self.wait(e, sem)


def build_program():
    nc = bass.Bass("TRN2", target_bir_lowering=False)

    d_x = nc.dram_tensor("x_loc", [TLOC, D], F32, kind="ExternalInput").ap()
    d_xhalo = nc.dram_tensor("x_halo", [D, 3], F32, kind="ExternalInput").ap()
    d_Win = nc.dram_tensor("W_in", [2 * DI, D], F32, kind="ExternalInput").ap()
    d_cw = nc.dram_tensor("conv_w", [DI, 4], F32, kind="ExternalInput").ap()
    d_cb = nc.dram_tensor("conv_b", [1, DI], F32, kind="ExternalInput").ap()
    d_Alog = nc.dram_tensor("A_log_dh", [DI // 2, DN], F32, kind="ExternalInput").ap()
    d_Wdt = nc.dram_tensor("W_dt", [DI, DTR], F32, kind="ExternalInput").ap()
    d_bdt = nc.dram_tensor("b_dt", [1, DI], F32, kind="ExternalInput").ap()
    d_WB = nc.dram_tensor("W_B", [DN, D], F32, kind="ExternalInput").ap()
    d_WC = nc.dram_tensor("W_C", [DN, D], F32, kind="ExternalInput").ap()
    d_Wout = nc.dram_tensor("W_out", [D, DI], F32, kind="ExternalInput").ap()
    d_wg = nc.dram_tensor("w_gate", [DFF, D], F32, kind="ExternalInput").ap()
    d_wu = nc.dram_tensor("w_up", [DFF, D], F32, kind="ExternalInput").ap()
    d_wd = nc.dram_tensor("w_down", [D, DFF], F32, kind="ExternalInput").ap()
    d_ident = nc.dram_tensor("ident", [128, 128], F32, kind="ExternalInput").ap()
    d_cst = nc.dram_tensor("csts", [128, 2], F32, kind="ExternalInput").ap()

    d_out = nc.dram_tensor("out_loc", [TLOC, D], F32, kind="ExternalOutput").ap()

    d_dt = nc.dram_tensor("dt_hbm", [NT, 128, NDT, TT], BF16, kind="Internal").ap()
    d_dx = nc.dram_tensor("dx_hbm", [NT, 128, NDT, TT], BF16, kind="Internal").ap()
    d_bc = nc.dram_tensor("bc_hbm", [NT, 2, DN, TT], BF16, kind="Internal").ap()
    d_zs = nc.dram_tensor("zs_hbm", [NT, 128, NDT, TT], BF16, kind="Internal").ap()
    d_dt_ag = nc.dram_tensor("dt_ag", [2, NT, 128, NDT, TT], BF16, kind="Internal").ap()
    d_dx_ag = nc.dram_tensor("dx_ag", [2, NT, 128, NDT, TT], BF16, kind="Internal").ap()
    d_bc_ag = nc.dram_tensor("bc_ag", [2, NT, 2, DN, TT], BF16, kind="Internal").ap()
    d_dt_my = nc.dram_tensor("dt_my", [NCH, 128, NDH, TT], BF16, kind="Internal").ap()
    d_dx_my = nc.dram_tensor("dx_my", [NCH, 128, NDH, TT], BF16, kind="Internal").ap()
    d_y = nc.dram_tensor("y_hbm", [NDH, 128, SEQ], BF16, kind="Internal").ap()
    d_y_ag = nc.dram_tensor("y_ag", [2, NDH, 128, SEQ], BF16, kind="Internal").ap()
    d_y_my = nc.dram_tensor("y_my", [2, NDH, 128, TLOC], BF16, kind="Internal").ap()
    d_x1 = nc.dram_tensor("x1_hbm", [NST, 128, D], F32, kind="Internal").ap()
    if _DBG:
        d_dbg = {k: nc.dram_tensor(f"dbg_{k}", shp, dt, kind="ExternalOutput").ap()
                 for k, shp, dt in (
                     ("dt", [NT, 128, NDT, TT], BF16),
                     ("dx", [NT, 128, NDT, TT], BF16),
                     ("zs", [NT, 128, NDT, TT], BF16),
                     ("bc", [NT, 2, DN, TT], BF16),
                     ("y", [NDH, 128, SEQ], BF16),
                     ("x1", [NST, 128, D], F32),
                     ("wgT", [128, NKD, DFF], BF16),
                     ("hst", [4, 128, NDH, DN], F32),
                     ("hall", [128, DN, TT], BF16),
                     ("brep", [2, 128, DN, TT], BF16),
                     ("crep", [2, 128, DN, TT], BF16),
                     ("bc2", [2, 33, 2, DN, TT], BF16),
                     ("ymy", [2, NDH, 128, TLOC], BF16),
                     ("ao", [NT, 128, NKD, TT], F32),
                     ("ysz", [NT, 128, NKI, TT], BF16),
                     ("xr", [NST, 128, D], F32),
                     ("lns", [NST, 128, 8], F32),
                 )}
    d_wgT = nc.dram_tensor("wgT_hbm", [128, NKD, DFF], BF16, kind="Internal").ap()
    d_wuT = nc.dram_tensor("wuT_hbm", [128, NKD, DFF], BF16, kind="Internal").ap()
    d_wdT = nc.dram_tensor("wdT_hbm", [128, NFF, D], BF16, kind="Internal").ap()

    p = Plan(nc)

    def SB(stack):
        def sb(name, shape, dtype):
            return stack.enter_context(nc.sbuf_tensor(name, shape, dtype))
        return sb

    def mk_quant(sb, pfx, Fmax, s_ve, s_ac):
        q_xn = sb(f"{pfx}_xn", [128, Fmax], F32)
        q_bm = sb(f"{pfx}_bm", [128, Fmax // 16], F32)
        q_sc = sb(f"{pfx}_sc", [128, Fmax // 16], F32)
        q_ma = sb(f"{pfx}_ma", [128, Fmax], BF16)
        q_mb = sb(f"{pfx}_mb", [128, Fmax], BF16)
        q_a1 = sb(f"{pfx}_a1", [128, Fmax], BF16)

        def quant(src_ap, F, out_ap):
            nb = F // 16
            p.op("vector", lambda E: E.reduce_max(
                out=q_bm[:, 0:nb], in_=src_ap.rearrange("p (g k) -> p g k", k=16),
                axis=AX.X, apply_absolute_value=True).then_inc(s_ve.h, 1)); s_ve.n += 1
            p.op("vector", lambda E: E.tensor_scalar(
                out=q_sc[:, 0:nb], in0=q_bm[:, 0:nb], scalar1=1.0 / 6.0, scalar2=1e-12,
                op0=ALU.mult, op1=ALU.max).then_inc(s_ve.h, 1)); s_ve.n += 1
            p.op("vector", lambda E: E.reciprocal(
                out=q_bm[:, 0:nb], in_=q_sc[:, 0:nb]).then_inc(s_ve.h, 1)); s_ve.n += 1
            p.wait("scalar", s_ve)
            p.op("scalar", lambda E: E.activation(
                out=q_xn[:, 0:F], in_=src_ap, func=AF.Abs).then_inc(s_ac.h, 1)); s_ac.n += 1
            p.wait("vector", s_ac)
            p.op("vector", lambda E: E.tensor_tensor(
                out=q_xn[:, 0:F].rearrange("p (g k) -> p g k", k=16),
                in0=q_xn[:, 0:F].rearrange("p (g k) -> p g k", k=16),
                in1=q_bm[:, 0:nb].unsqueeze(2).broadcast_to([128, nb, 16]),
                op=ALU.mult).then_inc(s_ve.h, 1)); s_ve.n += 1

            def msk(dst, mid, w):
                p.op("vector", lambda E, d=dst, mm=mid, ww=w: E.tensor_scalar(
                    out=d, in0=q_xn[:, 0:F], scalar1=mm, scalar2=ww,
                    op0=ALU.is_gt, op1=ALU.mult).then_inc(s_ve.h, 1)); s_ve.n += 1

            def tt(dst, a, b, opx=ALU.add):
                p.op("vector", lambda E, d=dst, aa=a, bb=b, oo=opx: E.tensor_tensor(
                    out=d, in0=aa, in1=bb, op=oo).then_inc(s_ve.h, 1)); s_ve.n += 1

            msk(q_a1[:, 0:F], QMIDS[0], QWS[0])
            msk(q_mb[:, 0:F], QMIDS[1], QWS[1])
            tt(q_a1[:, 0:F], q_a1[:, 0:F], q_mb[:, 0:F])
            for k in (2, 4, 6):
                msk(q_ma[:, 0:F], QMIDS[k], QWS[k])
                if k + 1 < 7:
                    msk(q_mb[:, 0:F], QMIDS[k + 1], QWS[k + 1])
                    tt(q_ma[:, 0:F], q_ma[:, 0:F], q_mb[:, 0:F])
                tt(q_a1[:, 0:F], q_a1[:, 0:F], q_ma[:, 0:F])
            p.op("vector", lambda E: E.tensor_scalar(
                out=q_ma[:, 0:F], in0=src_ap, scalar1=0.0, scalar2=2.0,
                op0=ALU.is_gt, op1=ALU.mult).then_inc(s_ve.h, 1)); s_ve.n += 1
            p.op("vector", lambda E: E.scalar_tensor_tensor(
                out=q_mb[:, 0:F].rearrange("p (g k) -> p g k", k=16),
                in0=q_ma[:, 0:F].rearrange("p (g k) -> p g k", k=16),
                scalar=-1.0,
                in1=q_sc[:, 0:nb].unsqueeze(2).broadcast_to([128, nb, 16]),
                op0=ALU.add, op1=ALU.mult).then_inc(s_ve.h, 1)); s_ve.n += 1
            tt(out_ap, q_a1[:, 0:F], q_mb[:, 0:F], ALU.mult)

        return quant

    es = contextlib.ExitStack()
    with es:
        sbg = SB(es)
        s_ld = Sem(es.enter_context(nc.semaphore("s_ld")), "s_ld")
        s_st = Sem(es.enter_context(nc.semaphore("s_st")), "s_st")
        s_pe = Sem(es.enter_context(nc.semaphore("s_pe")), "s_pe")
        s_ve = Sem(es.enter_context(nc.semaphore("s_ve")), "s_ve")
        s_ac = Sem(es.enter_context(nc.semaphore("s_ac")), "s_ac")
        s_cc = Sem(es.enter_context(nc.semaphore("s_cc")), "s_cc")
        s_gp = Sem(es.enter_context(nc.semaphore("s_gp")), "s_gp")
        s_bc = Sem(es.enter_context(nc.semaphore("s_bc")), "s_bc")
        s_ph = Sem(es.enter_context(nc.semaphore("s_ph")), "s_ph")

        def dma(eng, out, in_, s, slow=False):
            if slow:
                p.op(eng, lambda E, o=out, i=in_, ss=s: E.dma_start(
                    out=o, in_=i, allow_slow_non_contiguous=True).then_inc(ss.h, 16))
            else:
                p.op(eng, lambda E, o=out, i=in_, ss=s: E.dma_start(
                    out=o, in_=i).then_inc(ss.h, 16))
            s.n += 16

        ident_f = sbg("ident_f", [128, 128], F32)
        ident_bf = sbg("ident_bf", [128, 128], BF16)
        ones_bf = sbg("ones_bf", [33, 128], BF16)
        cw = sbg("cw", [128, NDT, 4], F32)
        cb = sbg("cb", [128, NDT], F32)
        bdt = sbg("bdt", [128, NDT], F32)
        A_sb = sbg("A_sb", [128, NDH, DN], F32)
        ln_s = sbg("ln_s", [128, 8], F32)
        cst_sb = sbg("cst_sb", [128, 2], F32)
        pbank = [es.enter_context(nc.psum_tensor(f"pb{i}", [128, 512], F32)) for i in range(6)]
        tbank = [es.enter_context(nc.psum_tensor(f"tb{i}", [128, 1024], BF16)) for i in range(2)]

        dma("sync", cst_sb[:], d_cst[:, :], s_ld)
        nc.const_aps.register(F32, 1e-5, cst_sb[:, 0:1])
        nc.const_aps.register(F32, -0.5, cst_sb[:, 1:2])
        dma("sync", ident_f[:], d_ident[:, :], s_ld)
        dma("sync", cw[:], d_cw.rearrange("(dt p) k -> p dt k", p=128), s_ld)
        dma("sync", cb[:], d_cb.rearrange("o (dt p) -> (o p) dt", p=128), s_ld, slow=True)
        dma("sync", bdt[:], d_bdt.rearrange("o (dt p) -> (o p) dt", p=128), s_ld, slow=True)
        dma("sync", A_sb[:], d_Alog.rearrange("(dt p) n -> p dt n", p=128), s_ld)
        p.wait("vector", s_ld)
        p.op("vector", lambda E: E.tensor_copy(out=ident_bf[:], in_=ident_f[:]).then_inc(s_ve.h, 1)); s_ve.n += 1
        p.op("vector", lambda E: E.memset(ones_bf[:], 1.0).then_inc(s_ve.h, 1)); s_ve.n += 1
        p.wait("scalar", s_ld)
        p.op("scalar", lambda E: E.activation(out=A_sb[:], in_=A_sb[:], func=AF.Exp).then_inc(s_ac.h, 1)); s_ac.n += 1
        p.wait("vector", s_ac)
        p.op("vector", lambda E: E.tensor_scalar(out=A_sb[:], in0=A_sb[:], scalar1=-1.0,
                                                 scalar2=None, op0=ALU.mult).then_inc(s_ve.h, 1)); s_ve.n += 1

        es_w = contextlib.ExitStack()
        sbw = SB(es_w)
        WtOut = sbw("WtOut", [128, NKI, D], BF16)

        es_a = contextlib.ExitStack()
        sba = SB(es_a)
        WtIn = sba("WtIn", [128, NKD, 2 * DI], BF16)
        WtDt = sba("WtDt", [64, DI], BF16)
        WtB = sba("WtB", [128, NKD, DN], BF16)
        WtC = sba("WtC", [128, NKD, DN], BF16)

        es_0 = contextlib.ExitStack()
        sb0 = SB(es_0)
        wstg_f = sb0("wstg_f", [128, 2, DFF], F32)
        wstg_bf = sb0("wstg_bf", [128, DFF], BF16)
        wT_out = sb0("wT_out", [128, NFF, 128], BF16)
        quant0 = mk_quant(sb0, "q0", DFF, s_ve, s_ac)

        def prep_wT(src_dram, M, K, put, quant):
            nmt = M // 128
            nkt = (K + 127) // 128
            for mt in range(nmt):
                p.wait("sync", s_ve)
                p.wait("sync", s_pe)
                p.wait("sync", s_ac)
                p.wait("vector", s_st)
                dma("sync", wstg_f[:, 0, 0:K], src_dram[mt * 128:(mt + 1) * 128, :], s_ld)
                p.wait("vector", s_ld)
                p.wait("scalar", s_ld)
                if quant:
                    quant0(wstg_f[:, 0, 0:K], K, wstg_bf[:, 0:K])
                else:
                    p.op("vector", lambda E: E.tensor_copy(
                        out=wstg_bf[:, 0:K], in_=wstg_f[:, 0, 0:K]).then_inc(s_ve.h, 1)); s_ve.n += 1
                p.wait("tensor", s_ve)
                for g in range((nkt + 7) // 8):
                    bk = "t%d" % (g % 2)
                    p.use_bank(bk)
                    kts = list(range(g * 8, min(g * 8 + 8, nkt)))
                    for kk, kt in enumerate(kts):
                        kw = min(128, K - kt * 128)
                        p.op("tensor", lambda E, b=tbank[g % 2], kk=kk, kt=kt, kw=kw: E.transpose(
                            b[0:kw, kk * 128:kk * 128 + 128],
                            wstg_bf[:, kt * 128:kt * 128 + kw],
                            ident_bf[:]).then_inc(s_pe.h, 1)); s_pe.n += 1
                    p.wait("vector", s_pe)
                    for kk, kt in enumerate(kts):
                        kw = min(128, K - kt * 128)
                        p.op("vector", lambda E, b=tbank[g % 2], kk=kk, kt=kt, kw=kw: E.tensor_copy(
                            out=wT_out[0:kw, kt, :],
                            in_=b[0:kw, kk * 128:kk * 128 + 128]).then_inc(s_ve.h, 1)); s_ve.n += 1
                    p.done_bank(bk, s_ve)
                p.wait("gpsimd", s_ve)
                put(mt)

        def put_win(mt):
            for kt in range(NKD):
                p.op("vector", lambda E, kt=kt, mt=mt: E.tensor_copy(
                    out=WtIn[:, kt, mt * 128:(mt + 1) * 128],
                    in_=wT_out[:, kt, :]).then_inc(s_ve.h, 1)); s_ve.n += 1
        prep_wT(d_Win, 2 * DI, D, put_win, quant=False)

        def put_wdt(mt):
            p.op("vector", lambda E, mt=mt: E.tensor_copy(
                out=WtDt[:, mt * 128:(mt + 1) * 128],
                in_=wT_out[0:64, 0, :]).then_inc(s_ve.h, 1)); s_ve.n += 1
        prep_wT(d_Wdt, DI, DTR, put_wdt, quant=False)

        def prep_bc(src_dram, dst):
            p.wait("sync", s_ve)
            p.wait("sync", s_pe)
            p.wait("sync", s_ac)
            dma("sync", wstg_f[0:DN, 0, 0:D], src_dram[:, :], s_ld)
            p.wait("vector", s_ld)
            p.op("vector", lambda E: E.tensor_copy(
                out=wstg_bf[0:DN, 0:D], in_=wstg_f[0:DN, 0, 0:D]).then_inc(s_ve.h, 1)); s_ve.n += 1
            p.wait("tensor", s_ve)
            for kt in range(NKD):
                bk = "t%d" % (kt % 2)
                p.use_bank(bk)
                p.op("tensor", lambda E, b=tbank[kt % 2], kt=kt: E.transpose(
                    b[:, 0:DN], wstg_bf[0:DN, kt * 128:(kt + 1) * 128],
                    ident_bf[0:DN, 0:DN]).then_inc(s_pe.h, 1)); s_pe.n += 1
                p.wait("vector", s_pe)
                p.op("vector", lambda E, b=tbank[kt % 2], kt=kt: E.tensor_copy(
                    out=dst[:, kt, :], in_=b[:, 0:DN]).then_inc(s_ve.h, 1)); s_ve.n += 1
                p.done_bank(bk, s_ve)
        prep_bc(d_WB, WtB)
        prep_bc(d_WC, WtC)

        def put_wout(mt):
            for kt in range(NKI):
                p.op("vector", lambda E, kt=kt, mt=mt: E.tensor_copy(
                    out=WtOut[:, kt, mt * 128:(mt + 1) * 128],
                    in_=wT_out[:, kt, :]).then_inc(s_ve.h, 1)); s_ve.n += 1
        prep_wT(d_Wout, D, DI, put_wout, quant=False)

        def put_ffn(dst_hbm, nkt):
            def f(mt):
                p.wait("gpsimd", s_ve)
                dma("gpsimd", dst_hbm[:, :, mt * 128:(mt + 1) * 128],
                    wT_out[:, 0:nkt, :], s_st)
            return f
        prep_wT(d_wg, DFF, D, put_ffn(d_wgT, NKD), quant=True)
        prep_wT(d_wu, DFF, D, put_ffn(d_wuT, NKD), quant=True)
        prep_wT(d_wd, D, DFF, put_ffn(d_wdT, NFF), quant=True)

        p.barrier(s_ph)
        p.flush("p0")
        es_0.close()

        es_1 = contextlib.ExitStack()
        sb1 = SB(es_1)
        xt_f = sb1("xt_f", [128, 2, D], F32)
        xt_bf = sb1("xt_bf", [128, 2, D], BF16)
        x_fm = sb1("x_fm", [128, NKD, TT], BF16)
        xval = sb1("xval", [128, NDT, TT + 3], BF16)
        zsil = sb1("zsil", [128, NDT, TT], BF16)
        dt_bf = sb1("dt_bf", [128, NDT, TT], BF16)
        dxv = sb1("dxv", [128, NDT, TT], BF16)
        xvs = sb1("xvs", [128, NDT, TT], BF16)
        csc = sb1("csc", [128, TT], F32)
        bcs = sb1("bcs", [DN, 2, TT], BF16)
        xh_f = sb1("xh_f", [128, NKD, 3], F32)
        xh_bf = sb1("xh_bf", [128, NKD, 3], BF16)

        dma("sync", xh_f[:], d_xhalo.rearrange("(k p) t -> p k t", p=128), s_ld)
        p.wait("vector", s_ld)
        p.op("vector", lambda E: E.tensor_copy(out=xh_bf[:], in_=xh_f[:]).then_inc(s_ve.h, 1)); s_ve.n += 1

        for i in range(NT):
            p.wait("vector", s_st)
            p.wait("scalar", s_st)
            for hf in range(2):
                p.wait("sync", s_ve)
                p.wait("vector", s_pe)
                dma("sync", xt_f[:], d_x[i * TT + hf * 256:i * TT + (hf + 1) * 256, :]
                    .rearrange("(s p) d -> p s d", p=128), s_ld)
                p.wait("vector", s_ld)
                p.op("vector", lambda E: E.tensor_copy(out=xt_bf[:], in_=xt_f[:]).then_inc(s_ve.h, 1)); s_ve.n += 1
                p.wait("tensor", s_ve)
                for ft in range(NKD):
                    bk = "t%d" % (ft % 2)
                    p.use_bank(bk)
                    for st in range(2):
                        p.op("tensor", lambda E, b=tbank[ft % 2], st=st, ft=ft: E.transpose(
                            b[:, st * 128:(st + 1) * 128],
                            xt_bf[:, st, ft * 128:(ft + 1) * 128],
                            ident_bf[:]).then_inc(s_pe.h, 1)); s_pe.n += 1
                    p.wait("vector", s_pe)
                    p.op("vector", lambda E, b=tbank[ft % 2], ft=ft, hf=hf: E.tensor_copy(
                        out=x_fm[:, ft, hf * 256:(hf + 1) * 256], in_=b[:, 0:256]).then_inc(s_ve.h, 1)); s_ve.n += 1
                    p.done_bank(bk, s_ve)

            if i == 0:
                p.wait("tensor", s_ve)
                for dt_i in range(NDT):
                    bk = 4 + (dt_i % 2)
                    p.use_bank(bk)
                    def hmm(E, b=pbank[bk], dt_i=dt_i):
                        for kt in range(NKD):
                            ins = E.matmul(b[:, 0:3], WtIn[:, kt, dt_i * 128:(dt_i + 1) * 128],
                                           xh_bf[:, kt, :], start=(kt == 0), stop=(kt == NKD - 1))
                            if kt == NKD - 1:
                                ins.then_inc(s_pe.h, 1)
                    p.op("tensor", hmm); s_pe.n += 1
                    p.wait("vector", s_pe)
                    p.op("vector", lambda E, b=pbank[bk], dt_i=dt_i: E.tensor_copy(
                        out=xval[:, dt_i, 0:3], in_=b[:, 0:3]).then_inc(s_ve.h, 1)); s_ve.n += 1
                    p.done_bank(bk, s_ve)
            else:
                p.op("vector", lambda E: E.tensor_copy(
                    out=xval[:, :, 0:3], in_=xval[:, :, TT:TT + 3]).then_inc(s_ve.h, 1)); s_ve.n += 1

            for ft in range(2 * NKI):
                bk = ft % 2
                p.use_bank(bk)
                def mmxz(E, b=pbank[bk], ft=ft):
                    for kt in range(NKD):
                        ins = E.matmul(b[:], WtIn[:, kt, ft * 128:(ft + 1) * 128],
                                       x_fm[:, kt, :], start=(kt == 0), stop=(kt == NKD - 1))
                        if kt == NKD - 1:
                            ins.then_inc(s_pe.h, 1)
                p.op("tensor", mmxz); s_pe.n += 1
                if ft < NDT:
                    p.wait("vector", s_pe)
                    p.op("vector", lambda E, b=pbank[bk], ft=ft: E.tensor_copy(
                        out=xval[:, ft, 3:TT + 3], in_=b[:]).then_inc(s_ve.h, 1)); s_ve.n += 1
                    p.done_bank(bk, s_ve)
                else:
                    p.wait("scalar", s_pe)
                    p.op("scalar", lambda E, b=pbank[bk], ft=ft: E.activation(
                        out=zsil[:, ft - NDT, :], in_=b[:], func=AF.Silu).then_inc(s_ac.h, 1)); s_ac.n += 1
                    p.done_bank(bk, s_ac)

            for dt_i in range(NDT):
                bk = 2 + (dt_i % 2)
                p.use_bank(bk)
                p.op("tensor", lambda E, b=pbank[bk], dt_i=dt_i: E.matmul(
                    b[:], WtDt[:, dt_i * 128:(dt_i + 1) * 128], x_fm[0:64, 0, :],
                    start=True, stop=True).then_inc(s_pe.h, 1)); s_pe.n += 1
                p.wait("scalar", s_pe)
                p.op("scalar", lambda E, b=pbank[bk], dt_i=dt_i: E.activation(
                    out=b[:], in_=b[:], func=AF.Exp, bias=bdt[:, dt_i:dt_i + 1]).then_inc(s_ac.h, 1)); s_ac.n += 1
                p.op("scalar", lambda E, b=pbank[bk], dt_i=dt_i: E.activation(
                    out=dt_bf[:, dt_i, :], in_=b[:], func=AF.Ln, bias=1.0).then_inc(s_ac.h, 1)); s_ac.n += 1
                p.done_bank(bk, s_ac)

            for which, wt in ((0, WtB), (1, WtC)):
                bk = 4 + which
                p.use_bank(bk)
                def mmbc(E, b=pbank[bk], wt=wt):
                    for kt in range(NKD):
                        ins = E.matmul(b[0:DN, :], wt[:, kt, :], x_fm[:, kt, :],
                                       start=(kt == 0), stop=(kt == NKD - 1))
                        if kt == NKD - 1:
                            ins.then_inc(s_pe.h, 1)
                p.op("tensor", mmbc); s_pe.n += 1
                p.wait("vector", s_pe)
                p.op("vector", lambda E, b=pbank[bk], w=which: E.tensor_copy(
                    out=bcs[:, w, :], in_=b[0:DN, :]).then_inc(s_ve.h, 1)); s_ve.n += 1
                p.done_bank(bk, s_ve)

            for dt_i in range(NDT):
                p.op("vector", lambda E, d=dt_i: E.tensor_scalar(
                    out=csc[:], in0=xval[:, d, 0:TT], scalar1=cw[:, d, 0:1],
                    scalar2=None, op0=ALU.mult).then_inc(s_ve.h, 1)); s_ve.n += 1
                for k in (1, 2, 3):
                    p.op("vector", lambda E, d=dt_i, k=k: E.scalar_tensor_tensor(
                        out=csc[:], in0=xval[:, d, k:TT + k], scalar=cw[:, d, k:k + 1],
                        in1=csc[:], op0=ALU.mult, op1=ALU.add).then_inc(s_ve.h, 1)); s_ve.n += 1
                p.wait("scalar", s_ve)
                p.op("scalar", lambda E, d=dt_i: E.activation(
                    out=xvs[:, d, :], in_=csc[:], func=AF.Silu,
                    bias=cb[:, d:d + 1]).then_inc(s_ac.h, 1)); s_ac.n += 1
                p.wait("vector", s_ac)
                p.op("vector", lambda E, d=dt_i: E.tensor_tensor(
                    out=dxv[:, d, :], in0=dt_bf[:, d, :], in1=xvs[:, d, :],
                    op=ALU.mult).then_inc(s_ve.h, 1)); s_ve.n += 1

            p.wait("gpsimd", s_ve)
            p.wait("gpsimd", s_ac)
            dma("gpsimd", d_dt[i], dt_bf[:], s_st)
            dma("gpsimd", d_dx[i], dxv[:], s_st)
            dma("gpsimd", d_zs[i], zsil[:], s_st)
            dma("gpsimd", d_bc[i, 0], bcs[:, 0, :], s_st)
            dma("gpsimd", d_bc[i, 1], bcs[:, 1, :], s_st)

        rg = [[0, 1], [2, 3], [4, 5], [6, 7]]
        p.wait("gpsimd", s_st)
        if _DBG:
            dma("gpsimd", d_dbg["dt"][:], d_dt[:], s_st)
            dma("gpsimd", d_dbg["dx"][:], d_dx[:], s_st)
            dma("gpsimd", d_dbg["zs"][:], d_zs[:], s_st)
            dma("gpsimd", d_dbg["bc"][:], d_bc[:], s_st)
            dma("gpsimd", d_dbg["wgT"][:], d_wgT[:], s_st)
        for a, b in ((d_dt, d_dt_ag), (d_dx, d_dx_ag), (d_bc, d_bc_ag)):
            p.op("gpsimd", lambda E, a=a, b=b: E.collective_compute(
                "AllGather", ALU.bypass, replica_groups=rg,
                ins=[a[:]], outs=[b[:]]).then_inc(s_cc.h, 1)); s_cc.n += 1
        p.barrier(s_ph)
        p.flush("p1a")
        es_1.close()
        es_a.close()

        _ccv = s_cc.n
        for arm in nc.switch_core_id(engines=[nc.sync], n=128):
            h = (arm.logical % 8) & 1
            nc.sync.wait_ge(s_cc.h, _ccv)
            nc.sync.dma_start(
                out=d_dt_my[:],
                in_=d_dt_ag.rearrange("w t p d c -> (w t) p d c")[:, :, 8 * h:8 * h + 8, :]
            ).then_inc(s_ld.h, 16)
            nc.sync.dma_start(
                out=d_dx_my[:],
                in_=d_dx_ag.rearrange("w t p d c -> (w t) p d c")[:, :, 8 * h:8 * h + 8, :]
            ).then_inc(s_ld.h, 16)
        s_ld.n += 32
        p.wm["sync"][s_cc.name] = _ccv
        restage_tok = p.tok(s_ld)

        es_b = contextlib.ExitStack()
        sbb = SB(es_b)
        bc2 = sbb("bc2", [33, 2, DN, TT], BF16)
        Brep = sbb("Brep", [128, DN, TT], BF16)
        Crep = sbb("Crep", [128, DN, TT], BF16)
        dts = sbb("dts", [128, 2, NDH, TT], BF16)
        dxs = sbb("dxs", [128, 2, NDH, TT], BF16)
        dA_f = sbb("dA_f", [128, 4, TT], F32)
        u_all = sbb("u_all", [128, DN, TT], BF16)
        h_all = sbb("h_all", [128, DN, TT], BF16)
        hc_all = sbb("hc_all", [128, DN, TT], BF16)
        hstate = sbb("hstate", [128, NDH, DN], F32)
        y_sb = sbb("y_sb", [128, 2, TT], BF16)
        ysb_tok = [None, None]

        p.wait("gpsimd", s_cc)
        p.wait("vector", s_cc)
        p.op("vector", lambda E: E.memset(hstate[:], 0.0).then_inc(s_ve.h, 1)); s_ve.n += 1

        slice_toks = {}
        bcl_toks = {}
        scan_hist = [None] * 16
        for c in range(NCH):
            t0 = c * TT
            hh, tt_i = divmod(c, NT)
            bsl = c % 2
            tokb = bcl_toks.get(c - 2)
            if tokb is not None:
                p.wait("gpsimd", tokb[0], tokb[1])
            dma("gpsimd", bc2[0:1, bsl].rearrange("o n t -> o (n t)"),
                d_bc_ag[hh, tt_i, 0].rearrange("n t -> () (n t)"), s_bc)
            dma("gpsimd", bc2[32:33, bsl].rearrange("o n t -> o (n t)"),
                d_bc_ag[hh, tt_i, 1].rearrange("n t -> () (n t)"), s_bc)
            bc_tok = p.tok(s_bc)
            p.wait("tensor", s_bc, bc_tok[1])
            for n in range(DN):
                bk = n % 3
                p.use_bank(bk)
                p.use_bank(3 + bk)
                def bcmm(E, b=pbank[bk], b2=pbank[3 + bk], n=n, bsl=bsl):
                    E.matmul(b[:], ones_bf[0:1, :], bc2[0:1, bsl, n, :],
                             start=True, stop=True)
                    E.matmul(b2[:], ones_bf[32:33, :], bc2[32:33, bsl, n, :],
                             start=True, stop=True).then_inc(s_pe.h, 1)
                p.op("tensor", bcmm); s_pe.n += 1
                p.wait("vector", s_pe)
                p.op("vector", lambda E, b=pbank[bk], n=n: E.tensor_copy(
                    out=Brep[:, n, :], in_=b[:]).then_inc(s_ve.h, 1)); s_ve.n += 1
                p.op("vector", lambda E, b2=pbank[3 + bk], n=n: E.tensor_copy(
                    out=Crep[:, n, :], in_=b2[:]).then_inc(s_ve.h, 1)); s_ve.n += 1
                p.done_bank(bk, s_ve)
                p.done_bank(3 + bk, s_ve)
            rep_tok = p.tok(s_ve)
            bcl_toks[c] = p.tok(s_pe)
            if _DBG and c < 2:
                p.wait("gpsimd", s_ve)
                dma("gpsimd", d_dbg["brep"][c], Brep[:], s_st)
                dma("gpsimd", d_dbg["crep"][c], Crep[:], s_st)
                dma("gpsimd", d_dbg["bc2"][c], bc2[:], s_st)
                bcl_toks[c] = (s_st, s_st.n)

            cb_i = c % 2
            p.wait_tok("sync", restage_tok)
            tokp = slice_toks.get(c - 2)
            if tokp is not None:
                p.wait("sync", tokp[0], tokp[1])
            dma("sync", dts[:, cb_i, :, :], d_dt_my[c], s_ld)
            dma("sync", dxs[:, cb_i, :, :], d_dx_my[c], s_ld)
            ld_tok = p.tok(s_ld)

            for j in range(NDH):
                p.wait_tok("vector", ld_tok)
                p.wait_tok("vector", rep_tok)
                p.op("vector", lambda E, cb_i=cb_i, j=j: E.tensor_tensor(
                    out=u_all[:],
                    in0=dxs[:, cb_i, j, :].unsqueeze(1).broadcast_to([128, DN, TT]),
                    in1=Brep[:], op=ALU.mult).then_inc(s_ve.h, 1)); s_ve.n += 1
                p.wait("vector", s_gp)
                for n in range(DN):
                    p.wait("scalar", s_ld, ld_tok[1])
                    if scan_hist[(n + 12) % 16] is not None:
                        p.wait("scalar", s_ve, scan_hist[(n + 12) % 16])
                    p.op("scalar", lambda E, cb_i=cb_i, j=j, n=n: E.activation(
                        out=dA_f[:, n % 4, :], in_=dts[:, cb_i, j, :], func=AF.Exp,
                        scale=A_sb[:, j, n:n + 1]).then_inc(s_ac.h, 1)); s_ac.n += 1
                    p.wait("vector", s_ac)
                    p.op("vector", lambda E, j=j, n=n: E.tensor_tensor_scan(
                        out=h_all[:, n, :], data0=dA_f[:, n % 4, :], data1=u_all[:, n, :],
                        initial=hstate[:, j, n:n + 1], op0=ALU.mult,
                        op1=ALU.add).then_inc(s_ve.h, 1)); s_ve.n += 1
                    scan_hist[n] = s_ve.n
                p.wait("scalar", s_ve)
                p.op("scalar", lambda E, j=j: E.activation(
                    out=hstate[:, j, :], in_=h_all[:, :, TT - 1],
                    func=AF.Copy).then_inc(s_gp.h, 1)); s_gp.n += 1
                p.op("vector", lambda E: E.tensor_tensor(
                    out=hc_all[:], in0=h_all[:], in1=Crep[:], op=ALU.mult).then_inc(s_ve.h, 1)); s_ve.n += 1
                p.op("vector", lambda E: E.tensor_tensor(
                    out=hc_all[:, 0:8, :], in0=hc_all[:, 0:8, :], in1=hc_all[:, 8:16, :],
                    op=ALU.add).then_inc(s_ve.h, 1)); s_ve.n += 1
                p.op("vector", lambda E: E.tensor_tensor(
                    out=hc_all[:, 0:4, :], in0=hc_all[:, 0:4, :], in1=hc_all[:, 4:8, :],
                    op=ALU.add).then_inc(s_ve.h, 1)); s_ve.n += 1
                p.op("vector", lambda E: E.tensor_tensor(
                    out=hc_all[:, 0:2, :], in0=hc_all[:, 0:2, :], in1=hc_all[:, 2:4, :],
                    op=ALU.add).then_inc(s_ve.h, 1)); s_ve.n += 1
                jb2 = j % 2
                if ysb_tok[jb2] is not None:
                    p.wait("vector", ysb_tok[jb2][0], ysb_tok[jb2][1])
                p.op("vector", lambda E, jb2=jb2: E.tensor_tensor(
                    out=y_sb[:, jb2, :], in0=hc_all[:, 0, :], in1=hc_all[:, 1, :],
                    op=ALU.add).then_inc(s_ve.h, 1)); s_ve.n += 1
                p.wait("gpsimd", s_ve)
                dma("gpsimd", d_y[j, :, t0:t0 + TT], y_sb[:, jb2, :], s_st)
                ysb_tok[jb2] = p.tok(s_st)
            if _DBG and c < 4:
                p.wait("gpsimd", s_gp)
                dma("gpsimd", d_dbg["hst"][c], hstate[:], s_st)
                if c == 0:
                    dma("gpsimd", d_dbg["hall"][:], h_all[:], s_st)
            slice_toks[c] = (s_st, s_st.n)

        p.wait("gpsimd", s_st)
        p.op("gpsimd", lambda E: E.collective_compute(
            "AllGather", ALU.bypass, replica_groups=rg,
            ins=[d_y[:]], outs=[d_y_ag[:]]).then_inc(s_cc.h, 1)); s_cc.n += 1
        if _DBG:
            dma("gpsimd", d_dbg["y"][:], d_y[:], s_st)
        p.barrier(s_ph)
        p.flush("p1b")
        es_b.close()

        _ccv2 = s_cc.n
        for arm in nc.switch_core_id(engines=[nc.sync], n=128):
            h = (arm.logical % 8) & 1
            nc.sync.wait_ge(s_cc.h, _ccv2)
            nc.sync.dma_start(
                out=d_y_my[:],
                in_=d_y_ag[:, :, :, h * TLOC:(h + 1) * TLOC]).then_inc(s_ld.h, 16)
        s_ld.n += 16
        p.wm["sync"][s_cc.name] = _ccv2
        yres_tok = p.tok(s_ld)

        es_c = contextlib.ExitStack()
        sbc = SB(es_c)
        ysz = sbc("ysz", [128, NKI, TT], BF16)
        zs2 = sbc("zs2", [128, NKI, TT], BF16)
        ao_f = sbc("ao_f", [128, NKD, TT], F32)
        aot_f = sbc("aot_f", [128, D], F32)
        xr_f = sbc("xr_f", [128, D], F32)
        x1_f = sbc("x1_f", [128, 4, D], F32)
        sq_f = sbc("sq_f", [128, D], F32)

        def ln_norm(src_col_ap, ob_src, dst_fn):
            pass

        if _DBG:
            p.wait_tok("gpsimd", yres_tok)
            dma("gpsimd", d_dbg["ymy"][:], d_y_my[:], s_st)
        for i in range(NT):
            p.wait_tok("sync", yres_tok)
            p.wait("sync", s_ve)
            p.wait("vector", s_st)
            for jg in range(NDT):
                dma("sync", ysz[:, jg, :], d_y_my[jg // NDH, jg % NDH, :, i * TT:(i + 1) * TT], s_ld)
            dma("sync", zs2[:], d_zs[i], s_ld)
            p.wait("vector", s_ld)
            p.op("vector", lambda E: E.tensor_tensor(
                out=ysz[:], in0=ysz[:], in1=zs2[:], op=ALU.mult).then_inc(s_ve.h, 1)); s_ve.n += 1
            p.wait("tensor", s_ve)
            for ft in range(NKD):
                bk = ft % 2
                p.use_bank(bk)
                def mmo(E, b=pbank[bk], ft=ft):
                    for kt in range(NKI):
                        ins = E.matmul(b[:], WtOut[:, kt, ft * 128:(ft + 1) * 128],
                                       ysz[:, kt, :], start=(kt == 0), stop=(kt == NKI - 1))
                        if kt == NKI - 1:
                            ins.then_inc(s_pe.h, 1)
                p.op("tensor", mmo); s_pe.n += 1
                p.wait("vector", s_pe)
                p.op("vector", lambda E, b=pbank[bk], ft=ft: E.tensor_copy(
                    out=ao_f[:, ft, :], in_=b[:]).then_inc(s_ve.h, 1)); s_ve.n += 1
                p.done_bank(bk, s_ve)

            if _DBG:
                p.wait("gpsimd", s_ve)
                dma("gpsimd", d_dbg["ao"][i], ao_f[:], s_st)
                dma("gpsimd", d_dbg["ysz"][i], ysz[:], s_st)
            for st in range(4):
                p.wait("tensor", s_ve)
                p.use_bank(2)
                p.use_bank(3)
                def tx(E, st=st):
                    for ft in range(NKD):
                        b = pbank[2] if ft < 4 else pbank[3]
                        col = (ft % 4) * 128
                        ins = E.transpose(b[:, col:col + 128],
                                          ao_f[:, ft, st * 128:(st + 1) * 128], ident_f[:])
                        if ft == NKD - 1:
                            ins.then_inc(s_pe.h, 1)
                p.op("tensor", tx); s_pe.n += 1
                p.wait("sync", s_ve)
                dma("sync", xr_f[:], d_x[i * TT + st * 128:i * TT + (st + 1) * 128, :], s_ld)
                p.wait("vector", s_pe)
                p.wait("vector", s_ld)
                p.op("vector", lambda E: E.tensor_copy(
                    out=aot_f[:, 0:512], in_=pbank[2][:]).then_inc(s_ve.h, 1)); s_ve.n += 1
                p.op("vector", lambda E: E.tensor_copy(
                    out=aot_f[:, 512:1024], in_=pbank[3][:]).then_inc(s_ve.h, 1)); s_ve.n += 1
                p.done_bank(2, s_ve)
                p.done_bank(3, s_ve)
                p.op("vector", lambda E: E.tensor_tensor(
                    out=xr_f[:], in0=xr_f[:], in1=aot_f[:], op=ALU.add).then_inc(s_ve.h, 1)); s_ve.n += 1
                p.op("vector", lambda E: E.reduce_sum(
                    out=ln_s[:, 0:1], in_=xr_f[:], axis=AX.X).then_inc(s_ve.h, 1)); s_ve.n += 1
                p.wait("scalar", s_ve)
                p.op("scalar", lambda E: E.activation(
                    out=sq_f[:], in_=xr_f[:], func=AF.Square).then_inc(s_ac.h, 1)); s_ac.n += 1
                p.wait("vector", s_ac)
                p.op("vector", lambda E: E.reduce_sum(
                    out=ln_s[:, 1:2], in_=sq_f[:], axis=AX.X).then_inc(s_ve.h, 1)); s_ve.n += 1
                p.wait("gpsimd", s_ve)
                p.op("gpsimd", lambda E: E.tensor_scalar(
                    out=ln_s[:, 2:3], in0=ln_s[:, 0:1], scalar1=1.0 / D, scalar2=None,
                    op0=ALU.mult)); 
                p.op("gpsimd", lambda E: E.tensor_tensor(
                    out=ln_s[:, 3:4], in0=ln_s[:, 2:3], in1=ln_s[:, 2:3], op=ALU.mult))
                p.op("gpsimd", lambda E: E.tensor_scalar(
                    out=ln_s[:, 4:5], in0=ln_s[:, 1:2], scalar1=1.0 / D, scalar2=None,
                    op0=ALU.mult))
                p.op("gpsimd", lambda E: E.tensor_tensor(
                    out=ln_s[:, 5:6], in0=ln_s[:, 4:5], in1=ln_s[:, 3:4], op=ALU.subtract))
                p.op("gpsimd", lambda E: E.tensor_scalar(
                    out=ln_s[:, 5:6], in0=ln_s[:, 5:6], scalar1=1e-5, scalar2=None,
                    op0=ALU.add).then_inc(s_gp.h, 1)); s_gp.n += 1
                p.wait("scalar", s_gp)
                p.op("scalar", lambda E: E.activation(
                    out=ln_s[:, 6:7], in_=ln_s[:, 5:6], func=AF.Ln).then_inc(s_ac.h, 1)); s_ac.n += 1
                p.op("scalar", lambda E: E.mul(out=ln_s[:, 6:7], in_=ln_s[:, 6:7], mul=-0.5).then_inc(s_ac.h, 1)); s_ac.n += 1
                p.op("scalar", lambda E: E.activation(
                    out=ln_s[:, 7:8], in_=ln_s[:, 6:7], func=AF.Exp).then_inc(s_ac.h, 1)); s_ac.n += 1
                p.wait("vector", s_ac)
                p.op("vector", lambda E, st=st: E.tensor_scalar(
                    out=x1_f[:, st, :], in0=xr_f[:], scalar1=ln_s[:, 2:3],
                    scalar2=ln_s[:, 7:8], op0=ALU.subtract, op1=ALU.mult).then_inc(s_ve.h, 1)); s_ve.n += 1
                if _DBG:
                    p.wait("gpsimd", s_ve)
                    dma("gpsimd", d_dbg["xr"][i * 4 + st], xr_f[:], s_st)
                    dma("gpsimd", d_dbg["lns"][i * 4 + st], ln_s[:], s_st)
            p.wait("gpsimd", s_ve)
            for st in range(4):
                dma("gpsimd", d_x1[i * 4 + st], x1_f[:, st, :], s_st)
        if _DBG:
            p.wait("gpsimd", s_st)
            dma("gpsimd", d_dbg["x1"][:], d_x1[:], s_st)
        p.barrier(s_ph)
        p.flush("p1c")
        es_c.close()
        es_w.close()

        es_2 = contextlib.ExitStack()
        sb2 = SB(es_2)
        WtG = sb2("WtG", [128, NKD, DFF], BF16)
        WtU_s = sb2("WtU_s", [128, 2, NKD, 128], BF16)
        WtD_s = sb2("WtD_s", [128, 2, NFF, 128], BF16)
        x1t = sb2("x1t", [128, 2, D], F32)
        x1r = sb2("x1r", [128, 2, D], F32)
        xq_tm = sb2("xq_tm", [128, 4, D], BF16)
        xq_fm = sb2("xq_fm", [128, NKD, TT], BF16)
        sg_t = sb2("sg_t", [128, TT], BF16)
        h_fm = sb2("h_fm", [128, NFF, TT], BF16)
        h_tm = sb2("h_tm", [128, DFF], BF16)
        do_f = sb2("do_f", [128, NKD, TT], F32)
        o_f = sb2("o_f", [128, 2, D], F32)
        sq_f = sb2("sq_f2", [128, D], F32)
        of_tok = [None, None]
        quant2 = mk_quant(sb2, "q2", DFF, s_ve, s_ac)

        p.wait("sync", s_st)
        dma("sync", WtG[:], d_wgT[:], s_ld)
        wld_tok = p.tok(s_ld)

        for i in range(NT):
            for st in range(4):
                stb = st % 2
                p.wait("sync", s_ve)
                p.wait("sync", s_ac)
                dma("sync", x1t[:, stb, :], d_x1[i * 4 + st], s_ld)
                p.wait("vector", s_ld)
                p.wait("scalar", s_ld)
                quant2(x1t[:, stb, :], D, xq_tm[:, st, :])
            p.wait("tensor", s_ve)
            for ft in range(NKD):
                bk = "t%d" % (ft % 2)
                p.use_bank(bk)
                def txq(E, ft=ft):
                    for st in range(4):
                        ins = E.transpose(tbank[ft % 2][:, st * 128:(st + 1) * 128],
                                          xq_tm[:, st, ft * 128:(ft + 1) * 128], ident_bf[:])
                        if st == 3:
                            ins.then_inc(s_pe.h, 1)
                p.op("tensor", txq); s_pe.n += 1
                p.wait("vector", s_pe)
                p.op("vector", lambda E, b=ft % 2, ft=ft: E.tensor_copy(
                    out=xq_fm[:, ft, :], in_=tbank[b][:, 0:512]).then_inc(s_ve.h, 1)); s_ve.n += 1
                p.done_bank(bk, s_ve)
            xq_tok = p.tok(s_ve)

            p.wait_tok("tensor", xq_tok)
            p.wait_tok("tensor", wld_tok)
            wu_ld = {}
            wu_mmtok = {}
            for ft in range(NFF + 1):
                if ft < NFF:
                    wub_n = ft % 2
                    if ft - 2 in wu_mmtok:
                        p.wait("sync", wu_mmtok[ft - 2][0], wu_mmtok[ft - 2][1])
                    else:
                        p.wait("sync", s_pe)
                    dma("sync", WtU_s[:, wub_n, :, :], d_wuT[:, :, ft * 128:(ft + 1) * 128], s_ld)
                    wu_ld[ft] = p.tok(s_ld)
                if ft == 0:
                    continue
                ft = ft - 1
                bkg, bku = 2 + (ft % 2), 4 + (ft % 2)
                wub = ft % 2
                p.wait("tensor", wu_ld[ft][0], wu_ld[ft][1])
                p.use_bank(bkg)
                p.use_bank(bku)
                def mmgu(E, ft=ft, bkg=bkg, bku=bku, wub=wub):
                    for kt in range(NKD):
                        E.matmul(pbank[bkg][:], WtG[:, kt, ft * 128:(ft + 1) * 128],
                                 xq_fm[:, kt, :], start=(kt == 0), stop=(kt == NKD - 1))
                    for kt in range(NKD):
                        ins = E.matmul(pbank[bku][:], WtU_s[:, wub, kt, :],
                                       xq_fm[:, kt, :], start=(kt == 0), stop=(kt == NKD - 1))
                        if kt == NKD - 1:
                            ins.then_inc(s_pe.h, 1)
                p.op("tensor", mmgu); s_pe.n += 1
                wu_mmtok[ft] = p.tok(s_pe)
                p.wait("scalar", s_pe)
                p.op("scalar", lambda E, bkg=bkg: E.activation(
                    out=sg_t[:], in_=pbank[bkg][:], func=AF.Silu).then_inc(s_ac.h, 1)); s_ac.n += 1
                p.done_bank(bkg, s_ac)
                p.wait("vector", s_ac)
                p.op("vector", lambda E, bku=bku, ft=ft: E.tensor_tensor(
                    out=h_fm[:, ft, :], in0=sg_t[:], in1=pbank[bku][:],
                    op=ALU.mult).then_inc(s_ve.h, 1)); s_ve.n += 1
                p.done_bank(bku, s_ve)
            h_tok = p.tok(s_ve)

            for st in range(4):
                p.wait_tok("tensor", h_tok)
                for g in range(3):
                    bk = "t%d" % (g % 2)
                    p.use_bank(bk)
                    kts = list(range(g * 8, min(g * 8 + 8, NFF)))
                    def txh(E, st=st, kts=kts, g=g):
                        for kk, kt in enumerate(kts):
                            ins = E.transpose(tbank[g % 2][:, kk * 128:kk * 128 + 128],
                                              h_fm[:, kt, st * 128:(st + 1) * 128], ident_bf[:])
                            if kk == len(kts) - 1:
                                ins.then_inc(s_pe.h, 1)
                    p.op("tensor", txh); s_pe.n += 1
                    p.wait("vector", s_pe)
                    for kk, kt in enumerate(kts):
                        p.op("vector", lambda E, g=g, kk=kk, kt=kt: E.tensor_copy(
                            out=h_tm[:, kt * 128:(kt + 1) * 128],
                            in_=tbank[g % 2][:, kk * 128:kk * 128 + 128]).then_inc(s_ve.h, 1)); s_ve.n += 1
                    p.done_bank(bk, s_ve)
                p.wait("scalar", s_ve)
                quant2(h_tm[:], DFF, h_tm[:])
                p.wait("tensor", s_ve)
                for g in range(3):
                    bk = "t%d" % (g % 2)
                    p.use_bank(bk)
                    kts = list(range(g * 8, min(g * 8 + 8, NFF)))
                    def txhq(E, kts=kts, g=g):
                        for kk, kt in enumerate(kts):
                            ins = E.transpose(tbank[g % 2][:, kk * 128:kk * 128 + 128],
                                              h_tm[:, kt * 128:(kt + 1) * 128], ident_bf[:])
                            if kk == len(kts) - 1:
                                ins.then_inc(s_pe.h, 1)
                    p.op("tensor", txhq); s_pe.n += 1
                    p.wait("vector", s_pe)
                    p.op("vector", lambda E, g=g, kts=kts, st=st: E.tensor_copy(
                        out=h_fm[:, kts[0]:kts[0] + len(kts), st * 128:(st + 1) * 128],
                        in_=tbank[g % 2][:, 0:len(kts) * 128].rearrange(
                            "p (k c) -> p k c", c=128)).then_inc(s_ve.h, 1)); s_ve.n += 1
                    p.done_bank(bk, s_ve)
            hqf_tok = p.tok(s_ve)

            p.wait_tok("tensor", hqf_tok)
            wd_ld = {}
            wd_mmtok = {}
            for ft in range(NKD + 1):
                if ft < NKD:
                    wb_n = ft % 2
                    if ft - 2 in wd_mmtok:
                        p.wait("sync", wd_mmtok[ft - 2][0], wd_mmtok[ft - 2][1])
                    else:
                        p.wait("sync", s_pe)
                    dma("sync", WtD_s[:, wb_n, :, :], d_wdT[:, :, ft * 128:(ft + 1) * 128], s_ld)
                    wd_ld[ft] = p.tok(s_ld)
                if ft == 0:
                    continue
                ft = ft - 1
                wb = ft % 2
                p.wait("tensor", wd_ld[ft][0], wd_ld[ft][1])
                bk = 2 + (ft % 2)
                p.use_bank(bk)
                def mmd(E, ft=ft, bk=bk, wb=wb):
                    for kt in range(NFF):
                        ins = E.matmul(pbank[bk][:], WtD_s[:, wb, kt, :],
                                       h_fm[:, kt, :], start=(kt == 0), stop=(kt == NFF - 1))
                        if kt == NFF - 1:
                            ins.then_inc(s_pe.h, 1)
                p.op("tensor", mmd); s_pe.n += 1
                wd_mmtok[ft] = p.tok(s_pe)
                p.wait("vector", s_pe)
                p.op("vector", lambda E, bk=bk, ft=ft: E.tensor_copy(
                    out=do_f[:, ft, :], in_=pbank[bk][:]).then_inc(s_ve.h, 1)); s_ve.n += 1
                p.done_bank(bk, s_ve)

            for st in range(4):
                stb = st % 2
                p.wait("sync", s_ve)
                dma("sync", x1r[:, stb, :], d_x1[i * 4 + st], s_ld)
                p.wait("tensor", s_ve)
                p.use_bank(4)
                p.use_bank(5)
                def txd(E, st=st):
                    for ft in range(NKD):
                        b = pbank[4] if ft < 4 else pbank[5]
                        col = (ft % 4) * 128
                        ins = E.transpose(b[:, col:col + 128],
                                          do_f[:, ft, st * 128:(st + 1) * 128], ident_f[:])
                        if ft == NKD - 1:
                            ins.then_inc(s_pe.h, 1)
                p.op("tensor", txd); s_pe.n += 1
                p.wait("vector", s_pe)
                ob = st % 2
                if of_tok[ob] is not None:
                    p.wait("vector", of_tok[ob][0], of_tok[ob][1])
                p.op("vector", lambda E, ob=ob: E.tensor_copy(
                    out=o_f[:, ob, 0:512], in_=pbank[4][:]).then_inc(s_ve.h, 1)); s_ve.n += 1
                p.op("vector", lambda E, ob=ob: E.tensor_copy(
                    out=o_f[:, ob, 512:1024], in_=pbank[5][:]).then_inc(s_ve.h, 1)); s_ve.n += 1
                p.done_bank(4, s_ve)
                p.done_bank(5, s_ve)
                p.wait("vector", s_ld)
                p.op("vector", lambda E, stb=stb, ob=ob: E.tensor_tensor(
                    out=o_f[:, ob, :], in0=o_f[:, ob, :], in1=x1r[:, stb, :],
                    op=ALU.add).then_inc(s_ve.h, 1)); s_ve.n += 1
                p.op("vector", lambda E, ob=ob: E.reduce_sum(
                    out=ln_s[:, 0:1], in_=o_f[:, ob, :], axis=AX.X).then_inc(s_ve.h, 1)); s_ve.n += 1
                p.wait("scalar", s_ve)
                p.op("scalar", lambda E, ob=ob: E.activation(
                    out=sq_f[:], in_=o_f[:, ob, :], func=AF.Square).then_inc(s_ac.h, 1)); s_ac.n += 1
                p.wait("vector", s_ac)
                p.op("vector", lambda E: E.reduce_sum(
                    out=ln_s[:, 1:2], in_=sq_f[:], axis=AX.X).then_inc(s_ve.h, 1)); s_ve.n += 1
                p.wait("gpsimd", s_ve)
                p.op("gpsimd", lambda E: E.tensor_scalar(
                    out=ln_s[:, 2:3], in0=ln_s[:, 0:1], scalar1=1.0 / D, scalar2=None,
                    op0=ALU.mult)); 
                p.op("gpsimd", lambda E: E.tensor_tensor(
                    out=ln_s[:, 3:4], in0=ln_s[:, 2:3], in1=ln_s[:, 2:3], op=ALU.mult))
                p.op("gpsimd", lambda E: E.tensor_scalar(
                    out=ln_s[:, 4:5], in0=ln_s[:, 1:2], scalar1=1.0 / D, scalar2=None,
                    op0=ALU.mult))
                p.op("gpsimd", lambda E: E.tensor_tensor(
                    out=ln_s[:, 5:6], in0=ln_s[:, 4:5], in1=ln_s[:, 3:4], op=ALU.subtract))
                p.op("gpsimd", lambda E: E.tensor_scalar(
                    out=ln_s[:, 5:6], in0=ln_s[:, 5:6], scalar1=1e-5, scalar2=None,
                    op0=ALU.add).then_inc(s_gp.h, 1)); s_gp.n += 1
                p.wait("scalar", s_gp)
                p.op("scalar", lambda E: E.activation(
                    out=ln_s[:, 6:7], in_=ln_s[:, 5:6], func=AF.Ln).then_inc(s_ac.h, 1)); s_ac.n += 1
                p.op("scalar", lambda E: E.mul(out=ln_s[:, 6:7], in_=ln_s[:, 6:7], mul=-0.5).then_inc(s_ac.h, 1)); s_ac.n += 1
                p.op("scalar", lambda E: E.activation(
                    out=ln_s[:, 7:8], in_=ln_s[:, 6:7], func=AF.Exp).then_inc(s_ac.h, 1)); s_ac.n += 1
                p.wait("vector", s_ac)
                p.op("vector", lambda E, ob=ob: E.tensor_scalar(
                    out=o_f[:, ob, :], in0=o_f[:, ob, :], scalar1=ln_s[:, 2:3],
                    scalar2=ln_s[:, 7:8], op0=ALU.subtract, op1=ALU.mult).then_inc(s_ve.h, 1)); s_ve.n += 1
                p.wait("gpsimd", s_ve)
                dma("gpsimd", d_out[i * TT + st * 128:i * TT + (st + 1) * 128, :],
                    o_f[:, ob, :], s_st)
                of_tok[ob] = p.tok(s_st)

        p.wait("gpsimd", s_st)
        p.flush("p2")
        es_2.close()
    return nc


_NC_CACHE = []


def kernel(L0_attn_x, L0_attn_W_in, L0_attn_conv_w, L0_attn_conv_b, L0_attn_A_log,
           L0_attn_W_dt, L0_attn_b_dt, L0_attn_W_B, L0_attn_W_C, L0_attn_W_out,
           L0_ffn_w_gate, L0_ffn_w_up, L0_ffn_w_down):
    x = np.ascontiguousarray(np.asarray(L0_attn_x, dtype=np.float32))
    W_in = np.ascontiguousarray(L0_attn_W_in, dtype=np.float32)
    conv_w = np.ascontiguousarray(np.asarray(L0_attn_conv_w, dtype=np.float32).reshape(DI, 4))
    conv_b = np.ascontiguousarray(np.asarray(L0_attn_conv_b, dtype=np.float32).reshape(1, DI))
    A_log = np.ascontiguousarray(L0_attn_A_log, dtype=np.float32)
    W_dt = np.ascontiguousarray(L0_attn_W_dt, dtype=np.float32)
    b_dt = np.ascontiguousarray(np.asarray(L0_attn_b_dt, dtype=np.float32).reshape(1, DI))
    W_B = np.ascontiguousarray(L0_attn_W_B, dtype=np.float32)
    W_C = np.ascontiguousarray(L0_attn_W_C, dtype=np.float32)
    W_out = np.ascontiguousarray(L0_attn_W_out, dtype=np.float32)
    w_gate = np.ascontiguousarray(L0_ffn_w_gate, dtype=np.float32)
    w_up = np.ascontiguousarray(L0_ffn_w_up, dtype=np.float32)
    w_down = np.ascontiguousarray(L0_ffn_w_down, dtype=np.float32)
    ident = np.eye(128, dtype=np.float32)

    if not _NC_CACHE:
        _NC_CACHE.append(build_program())
    nc = _NC_CACHE[0]

    in_maps = []
    for r in range(8):
        s, h = r >> 1, r & 1
        halo = (np.zeros((D, 3), np.float32) if h == 0
                else np.ascontiguousarray(x[s, TLOC - 3:TLOC, :].T))
        in_maps.append({
            "x_loc": np.ascontiguousarray(x[s, h * TLOC:(h + 1) * TLOC, :]),
            "x_halo": halo,
            "W_in": W_in,
            "conv_w": conv_w,
            "conv_b": conv_b,
            "A_log_dh": np.ascontiguousarray(A_log[h * (DI // 2):(h + 1) * (DI // 2), :]),
            "W_dt": W_dt,
            "b_dt": b_dt,
            "W_B": W_B,
            "W_C": W_C,
            "W_out": W_out,
            "w_gate": w_gate,
            "w_up": w_up,
            "w_down": w_down,
            "ident": ident,
            "csts": np.tile(np.array([[1e-5, -0.5]], dtype=np.float32), (128, 1)),
        })
    res = run_bass_kernel_spmd(nc, in_maps, core_ids=list(range(8)))
    out = np.empty((4, SEQ, D), np.float32)
    for r in range(8):
        s, h = r >> 1, r & 1
        out[s, h * TLOC:(h + 1) * TLOC, :] = res.results[r]["out_loc"]
    return out


# version 09
# speedup vs baseline: 1.1379x; 1.0328x over previous
import contextlib
import numpy as np
import concourse.bass as bass
import concourse.mybir as mybir
from concourse.bass_utils import run_bass_kernel_spmd

F32 = mybir.dt.float32
BF16 = mybir.dt.bfloat16
AF = mybir.ActivationFunctionType
ALU = mybir.AluOpType
AX = mybir.AxisListType

D = 1024
DI = 2048
DN = 16
DTR = 64
DFF = 2816
TLOC = 4096
TT = 512
NT = TLOC // TT
SEQ = 8192
NDT = DI // 128
NDH = NDT // 2
NCH = SEQ // TT
NKD = D // 128
NKI = DI // 128
NFF = DFF // 128
NST = TLOC // 128

_DBG = False
QMIDS = (0.25, 0.75, 1.25, 1.75, 2.5, 3.5, 5.0)
QWS = (0.5, 0.5, 0.5, 0.5, 1.0, 1.0, 2.0)


class Sem:
    __slots__ = ("h", "name", "n")

    def __init__(self, h, name):
        self.h, self.name, self.n = h, name, 0


class Plan:
    ENGS = ("sync", "tensor", "scalar", "vector", "gpsimd")

    def __init__(self, nc):
        self.nc = nc
        self.q = {e: [] for e in self.ENGS}
        self.wm = {e: {} for e in self.ENGS}
        self.bank_tok = {}
        self.nblk = 0

    def op(self, eng, fn):
        self.q[eng].append(fn)

    def wait(self, eng, sem, val=None):
        v = sem.n if val is None else val
        if v <= 0:
            return
        if self.wm[eng].get(sem.name, -1) >= v:
            return
        self.wm[eng][sem.name] = v
        self.q[eng].append(lambda E, s=sem, v=v: E.wait_ge(s.h, v))

    def tok(self, sem):
        return (sem, sem.n)

    def wait_tok(self, eng, tok):
        if tok is not None:
            self.wait(eng, tok[0], tok[1])

    def use_bank(self, k):
        self.wait_tok("tensor", self.bank_tok.get(k))

    def done_bank(self, k, sem):
        self.bank_tok[k] = (sem, sem.n)

    def flush(self, name=None):
        qs = {e: self.q[e] for e in self.ENGS}
        self.q = {e: [] for e in self.ENGS}
        name = name or f"b{self.nblk}"
        self.nblk += 1
        if not any(qs.values()):
            return
        with self.nc.Block(name) as blk:
            for e in self.ENGS:
                fns = qs[e]
                if fns:
                    getattr(blk, e)(lambda E, fns=fns: [f(E) for f in fns])

    def barrier(self, sem):
        for e in self.ENGS:
            self.q[e].append(lambda E, s=sem: E.sem_inc(s.h, 1))
            sem.n += 1
        for e in self.ENGS:
            self.wait(e, sem)


def build_program():
    nc = bass.Bass("TRN2", target_bir_lowering=False)

    d_x = nc.dram_tensor("x_loc", [TLOC, D], F32, kind="ExternalInput").ap()
    d_xhalo = nc.dram_tensor("x_halo", [D, 3], F32, kind="ExternalInput").ap()
    d_Win = nc.dram_tensor("W_in", [2 * DI, D], F32, kind="ExternalInput").ap()
    d_cw = nc.dram_tensor("conv_w", [DI, 4], F32, kind="ExternalInput").ap()
    d_cb = nc.dram_tensor("conv_b", [1, DI], F32, kind="ExternalInput").ap()
    d_Alog = nc.dram_tensor("A_log_dh", [DI // 2, DN], F32, kind="ExternalInput").ap()
    d_Wdt = nc.dram_tensor("W_dt", [DI, DTR], F32, kind="ExternalInput").ap()
    d_bdt = nc.dram_tensor("b_dt", [1, DI], F32, kind="ExternalInput").ap()
    d_WB = nc.dram_tensor("W_B", [DN, D], F32, kind="ExternalInput").ap()
    d_WC = nc.dram_tensor("W_C", [DN, D], F32, kind="ExternalInput").ap()
    d_Wout = nc.dram_tensor("W_out", [D, DI], F32, kind="ExternalInput").ap()
    d_wg = nc.dram_tensor("w_gate", [DFF, D], F32, kind="ExternalInput").ap()
    d_wu = nc.dram_tensor("w_up", [DFF, D], F32, kind="ExternalInput").ap()
    d_wd = nc.dram_tensor("w_down", [D, DFF], F32, kind="ExternalInput").ap()
    d_ident = nc.dram_tensor("ident", [128, 128], F32, kind="ExternalInput").ap()
    d_cst = nc.dram_tensor("csts", [128, 2], F32, kind="ExternalInput").ap()

    d_out = nc.dram_tensor("out_loc", [TLOC, D], F32, kind="ExternalOutput").ap()

    d_dt = nc.dram_tensor("dt_hbm", [NT, 128, NDT, TT], BF16, kind="Internal").ap()
    d_dx = nc.dram_tensor("dx_hbm", [NT, 128, NDT, TT], BF16, kind="Internal").ap()
    d_bc = nc.dram_tensor("bc_hbm", [NT, 2, DN, TT], BF16, kind="Internal").ap()
    d_zs = nc.dram_tensor("zs_hbm", [NT, 128, NDT, TT], BF16, kind="Internal").ap()
    d_dt_ag = nc.dram_tensor("dt_ag", [2, NT, 128, NDT, TT], BF16, kind="Internal").ap()
    d_dx_ag = nc.dram_tensor("dx_ag", [2, NT, 128, NDT, TT], BF16, kind="Internal").ap()
    d_bc_ag = nc.dram_tensor("bc_ag", [2, NT, 2, DN, TT], BF16, kind="Internal").ap()
    d_dt_my = nc.dram_tensor("dt_my", [NCH, 128, NDH, TT], BF16, kind="Internal").ap()
    d_dx_my = nc.dram_tensor("dx_my", [NCH, 128, NDH, TT], BF16, kind="Internal").ap()
    d_y = nc.dram_tensor("y_hbm", [NDH, 128, SEQ], BF16, kind="Internal").ap()
    d_y_ag = nc.dram_tensor("y_ag", [2, NDH, 128, SEQ], BF16, kind="Internal").ap()
    d_y_my = nc.dram_tensor("y_my", [2, NDH, 128, TLOC], BF16, kind="Internal").ap()
    d_x1 = nc.dram_tensor("x1_hbm", [NST, 128, D], F32, kind="Internal").ap()
    if _DBG:
        d_dbg = {k: nc.dram_tensor(f"dbg_{k}", shp, dt, kind="ExternalOutput").ap()
                 for k, shp, dt in (
                     ("dt", [NT, 128, NDT, TT], BF16),
                     ("dx", [NT, 128, NDT, TT], BF16),
                     ("zs", [NT, 128, NDT, TT], BF16),
                     ("bc", [NT, 2, DN, TT], BF16),
                     ("y", [NDH, 128, SEQ], BF16),
                     ("x1", [NST, 128, D], F32),
                     ("wgT", [128, NKD, DFF], BF16),
                     ("hst", [4, 128, NDH, DN], F32),
                     ("hall", [128, DN, TT], BF16),
                     ("brep", [2, 128, DN, TT], BF16),
                     ("crep", [2, 128, DN, TT], BF16),
                     ("bc2", [2, 33, 2, DN, TT], BF16),
                     ("ymy", [2, NDH, 128, TLOC], BF16),
                     ("ao", [NT, 128, NKD, TT], F32),
                     ("ysz", [NT, 128, NKI, TT], BF16),
                     ("xr", [NST, 128, D], F32),
                     ("lns", [NST, 128, 8], F32),
                 )}
    d_wgT = nc.dram_tensor("wgT_hbm", [128, NKD, DFF], BF16, kind="Internal").ap()
    d_wuT = nc.dram_tensor("wuT_hbm", [128, NKD, DFF], BF16, kind="Internal").ap()
    d_wdT = nc.dram_tensor("wdT_hbm", [128, NFF, D], BF16, kind="Internal").ap()

    p = Plan(nc)

    def SB(stack):
        def sb(name, shape, dtype):
            return stack.enter_context(nc.sbuf_tensor(name, shape, dtype))
        return sb

    def mk_quant(sb, pfx, Fmax, s_ve, s_ac):
        q_xn = sb(f"{pfx}_xn", [128, Fmax], F32)
        q_bm = sb(f"{pfx}_bm", [128, Fmax // 16], F32)
        q_sc = sb(f"{pfx}_sc", [128, Fmax // 16], F32)
        q_ma = sb(f"{pfx}_ma", [128, Fmax], BF16)
        q_mb = sb(f"{pfx}_mb", [128, Fmax], BF16)
        q_a1 = sb(f"{pfx}_a1", [128, Fmax], BF16)
        q_rc = sb(f"{pfx}_rc", [128, Fmax // 16], F32)

        def quant(src_ap, F, out_ap):
            nb = F // 16
            p.op("vector", lambda E: E.reduce_max(
                out=q_bm[:, 0:nb], in_=src_ap.rearrange("p (g k) -> p g k", k=16),
                axis=AX.X, apply_absolute_value=True).then_inc(s_ve.h, 1)); s_ve.n += 1
            p.op("vector", lambda E: E.tensor_scalar(
                out=q_sc[:, 0:nb], in0=q_bm[:, 0:nb], scalar1=1.0 / 6.0, scalar2=1e-12,
                op0=ALU.mult, op1=ALU.max).then_inc(s_ve.h, 1)); s_ve.n += 1
            p.wait("scalar", s_ve)
            p.op("scalar", lambda E: E.activation(
                out=q_rc[:, 0:nb], in_=q_sc[:, 0:nb], func=AF.Ln).then_inc(s_ac.h, 1)); s_ac.n += 1
            p.op("scalar", lambda E: E.activation(
                out=q_rc[:, 0:nb], in_=q_rc[:, 0:nb], func=AF.Exp,
                scale=-1.0).then_inc(s_ac.h, 1)); s_ac.n += 1
            p.op("vector", lambda E: E.scalar_tensor_tensor(
                out=q_xn[:, 0:F], in0=src_ap, scalar=-1.0, in1=src_ap,
                op0=ALU.mult, op1=ALU.max).then_inc(s_ve.h, 1)); s_ve.n += 1
            p.wait("vector", s_ac)
            p.op("vector", lambda E: E.tensor_tensor(
                out=q_xn[:, 0:F].rearrange("p (g k) -> p g k", k=16),
                in0=q_xn[:, 0:F].rearrange("p (g k) -> p g k", k=16),
                in1=q_rc[:, 0:nb].unsqueeze(2).broadcast_to([128, nb, 16]),
                op=ALU.mult).then_inc(s_ve.h, 1)); s_ve.n += 1

            def msk(dst, mid, w):
                p.op("vector", lambda E, d=dst, mm=mid, ww=w: E.tensor_scalar(
                    out=d, in0=q_xn[:, 0:F], scalar1=mm, scalar2=ww,
                    op0=ALU.is_gt, op1=ALU.mult).then_inc(s_ve.h, 1)); s_ve.n += 1

            def tt(dst, a, b, opx=ALU.add):
                p.op("vector", lambda E, d=dst, aa=a, bb=b, oo=opx: E.tensor_tensor(
                    out=d, in0=aa, in1=bb, op=oo).then_inc(s_ve.h, 1)); s_ve.n += 1

            msk(q_a1[:, 0:F], QMIDS[0], QWS[0])
            msk(q_mb[:, 0:F], QMIDS[1], QWS[1])
            tt(q_a1[:, 0:F], q_a1[:, 0:F], q_mb[:, 0:F])
            for k in (2, 4, 6):
                msk(q_ma[:, 0:F], QMIDS[k], QWS[k])
                if k + 1 < 7:
                    msk(q_mb[:, 0:F], QMIDS[k + 1], QWS[k + 1])
                    tt(q_ma[:, 0:F], q_ma[:, 0:F], q_mb[:, 0:F])
                tt(q_a1[:, 0:F], q_a1[:, 0:F], q_ma[:, 0:F])
            p.op("vector", lambda E: E.tensor_scalar(
                out=q_ma[:, 0:F], in0=src_ap, scalar1=0.0, scalar2=2.0,
                op0=ALU.is_gt, op1=ALU.mult).then_inc(s_ve.h, 1)); s_ve.n += 1
            p.op("vector", lambda E: E.scalar_tensor_tensor(
                out=q_mb[:, 0:F].rearrange("p (g k) -> p g k", k=16),
                in0=q_ma[:, 0:F].rearrange("p (g k) -> p g k", k=16),
                scalar=-1.0,
                in1=q_sc[:, 0:nb].unsqueeze(2).broadcast_to([128, nb, 16]),
                op0=ALU.add, op1=ALU.mult).then_inc(s_ve.h, 1)); s_ve.n += 1
            tt(out_ap, q_a1[:, 0:F], q_mb[:, 0:F], ALU.mult)

        return quant

    es = contextlib.ExitStack()
    with es:
        sbg = SB(es)
        s_ld = Sem(es.enter_context(nc.semaphore("s_ld")), "s_ld")
        s_st = Sem(es.enter_context(nc.semaphore("s_st")), "s_st")
        s_pe = Sem(es.enter_context(nc.semaphore("s_pe")), "s_pe")
        s_ve = Sem(es.enter_context(nc.semaphore("s_ve")), "s_ve")
        s_ac = Sem(es.enter_context(nc.semaphore("s_ac")), "s_ac")
        s_cc = Sem(es.enter_context(nc.semaphore("s_cc")), "s_cc")
        s_gp = Sem(es.enter_context(nc.semaphore("s_gp")), "s_gp")
        s_bc = Sem(es.enter_context(nc.semaphore("s_bc")), "s_bc")
        s_ph = Sem(es.enter_context(nc.semaphore("s_ph")), "s_ph")

        def dma(eng, out, in_, s, slow=False):
            if slow:
                p.op(eng, lambda E, o=out, i=in_, ss=s: E.dma_start(
                    out=o, in_=i, allow_slow_non_contiguous=True).then_inc(ss.h, 16))
            else:
                p.op(eng, lambda E, o=out, i=in_, ss=s: E.dma_start(
                    out=o, in_=i).then_inc(ss.h, 16))
            s.n += 16

        ident_f = sbg("ident_f", [128, 128], F32)
        ident_bf = sbg("ident_bf", [128, 128], BF16)
        ones_bf = sbg("ones_bf", [33, 128], BF16)
        cw = sbg("cw", [128, NDT, 4], F32)
        cb = sbg("cb", [128, NDT], F32)
        bdt = sbg("bdt", [128, NDT], F32)
        A_sb = sbg("A_sb", [128, NDH, DN], F32)
        ln_s = sbg("ln_s", [128, 8], F32)
        cst_sb = sbg("cst_sb", [128, 2], F32)
        pbank = [es.enter_context(nc.psum_tensor(f"pb{i}", [128, 512], F32)) for i in range(6)]
        tbank = [es.enter_context(nc.psum_tensor(f"tb{i}", [128, 1024], BF16)) for i in range(2)]

        dma("sync", cst_sb[:], d_cst[:, :], s_ld)
        nc.const_aps.register(F32, 1e-5, cst_sb[:, 0:1])
        nc.const_aps.register(F32, -0.5, cst_sb[:, 1:2])
        dma("sync", ident_f[:], d_ident[:, :], s_ld)
        dma("sync", cw[:], d_cw.rearrange("(dt p) k -> p dt k", p=128), s_ld)
        dma("sync", cb[:], d_cb.rearrange("o (dt p) -> (o p) dt", p=128), s_ld, slow=True)
        dma("sync", bdt[:], d_bdt.rearrange("o (dt p) -> (o p) dt", p=128), s_ld, slow=True)
        dma("sync", A_sb[:], d_Alog.rearrange("(dt p) n -> p dt n", p=128), s_ld)
        p.wait("vector", s_ld)
        p.op("vector", lambda E: E.tensor_copy(out=ident_bf[:], in_=ident_f[:]).then_inc(s_ve.h, 1)); s_ve.n += 1
        p.op("vector", lambda E: E.memset(ones_bf[:], 1.0).then_inc(s_ve.h, 1)); s_ve.n += 1
        p.wait("scalar", s_ld)
        p.op("scalar", lambda E: E.activation(out=A_sb[:], in_=A_sb[:], func=AF.Exp).then_inc(s_ac.h, 1)); s_ac.n += 1
        p.wait("vector", s_ac)
        p.op("vector", lambda E: E.tensor_scalar(out=A_sb[:], in0=A_sb[:], scalar1=-1.0,
                                                 scalar2=None, op0=ALU.mult).then_inc(s_ve.h, 1)); s_ve.n += 1

        es_w = contextlib.ExitStack()
        sbw = SB(es_w)
        WtOut = sbw("WtOut", [128, NKI, D], BF16)

        es_a = contextlib.ExitStack()
        sba = SB(es_a)
        WtIn = sba("WtIn", [128, NKD, 2 * DI], BF16)
        WtDt = sba("WtDt", [64, DI], BF16)
        WtB = sba("WtB", [128, NKD, DN], BF16)
        WtC = sba("WtC", [128, NKD, DN], BF16)

        es_0 = contextlib.ExitStack()
        sb0 = SB(es_0)
        wstg_f = sb0("wstg_f", [128, 2, DFF], F32)
        wstg_bf = sb0("wstg_bf", [128, DFF], BF16)
        wT_out = sb0("wT_out", [128, NFF, 128], BF16)
        quant0 = mk_quant(sb0, "q0", DFF, s_ve, s_ac)

        wslot = [0]
        wstg_toks = [None, None]

        def prep_wT(src_dram, M, K, put, quant):
            nmt = M // 128
            nkt = (K + 127) // 128
            for mt in range(nmt):
                sl = wslot[0]; wslot[0] ^= 1
                if wstg_toks[sl] is not None:
                    p.wait("sync", wstg_toks[sl][0], wstg_toks[sl][1])
                dma("sync", wstg_f[:, sl, 0:K], src_dram[mt * 128:(mt + 1) * 128, :], s_ld)
                ldt = p.tok(s_ld)
                if quant:
                    p.wait("vector", s_ld, ldt[1])
                    p.wait("vector", s_pe)
                    p.wait("vector", s_st)
                    quant0(wstg_f[:, sl, 0:K], K, wstg_bf[:, 0:K])
                    wstg_toks[sl] = (s_ve, s_ve.n)
                    p.wait("tensor", s_ve)
                    for g in range((nkt + 7) // 8):
                        bk = "t%d" % (g % 2)
                        p.use_bank(bk)
                        kts = list(range(g * 8, min(g * 8 + 8, nkt)))
                        for kk, kt in enumerate(kts):
                            kw = min(128, K - kt * 128)
                            p.op("tensor", lambda E, b=tbank[g % 2], kk=kk, kt=kt, kw=kw: E.transpose(
                                b[0:kw, kk * 128:kk * 128 + 128],
                                wstg_bf[:, kt * 128:kt * 128 + kw],
                                ident_bf[:]).then_inc(s_pe.h, 1)); s_pe.n += 1
                        p.wait("vector", s_pe)
                        for kk, kt in enumerate(kts):
                            kw = min(128, K - kt * 128)
                            p.op("vector", lambda E, b=tbank[g % 2], kk=kk, kt=kt, kw=kw: E.tensor_copy(
                                out=wT_out[0:kw, kt, :],
                                in_=b[0:kw, kk * 128:kk * 128 + 128]).then_inc(s_ve.h, 1)); s_ve.n += 1
                        p.done_bank(bk, s_ve)
                    p.wait("gpsimd", s_ve)
                    put(mt)
                else:
                    p.wait("tensor", s_ld, ldt[1])
                    for g in range((nkt + 3) // 4):
                        bk = g % 2
                        p.use_bank(bk)
                        kts = list(range(g * 4, min(g * 4 + 4, nkt)))
                        for kk, kt in enumerate(kts):
                            kw = min(128, K - kt * 128)
                            p.op("tensor", lambda E, b=pbank[bk], kk=kk, kt=kt, kw=kw, sl=sl: E.transpose(
                                b[0:kw, kk * 128:kk * 128 + 128],
                                wstg_f[:, sl, kt * 128:kt * 128 + kw],
                                ident_f[:]).then_inc(s_pe.h, 1)); s_pe.n += 1
                        p.wait("vector", s_pe)
                        for kk, kt in enumerate(kts):
                            kw = min(128, K - kt * 128)
                            p.op("vector", lambda E, b=pbank[bk], kk=kk, kt=kt, kw=kw: E.tensor_copy(
                                out=wT_out[0:kw, kt, :],
                                in_=b[0:kw, kk * 128:kk * 128 + 128]).then_inc(s_ve.h, 1)); s_ve.n += 1
                        p.done_bank(bk, s_ve)
                    wstg_toks[sl] = (s_pe, s_pe.n)
                    put(mt)

        def put_win(mt):
            for kt in range(NKD):
                p.op("vector", lambda E, kt=kt, mt=mt: E.tensor_copy(
                    out=WtIn[:, kt, mt * 128:(mt + 1) * 128],
                    in_=wT_out[:, kt, :]).then_inc(s_ve.h, 1)); s_ve.n += 1
        prep_wT(d_Win, 2 * DI, D, put_win, quant=False)

        def put_wdt(mt):
            p.op("vector", lambda E, mt=mt: E.tensor_copy(
                out=WtDt[:, mt * 128:(mt + 1) * 128],
                in_=wT_out[0:64, 0, :]).then_inc(s_ve.h, 1)); s_ve.n += 1
        prep_wT(d_Wdt, DI, DTR, put_wdt, quant=False)

        def prep_bc(src_dram, dst):
            p.wait("sync", s_ve)
            p.wait("sync", s_pe)
            p.wait("sync", s_ac)
            dma("sync", wstg_f[0:DN, 0, 0:D], src_dram[:, :], s_ld)
            p.wait("vector", s_ld)
            p.op("vector", lambda E: E.tensor_copy(
                out=wstg_bf[0:DN, 0:D], in_=wstg_f[0:DN, 0, 0:D]).then_inc(s_ve.h, 1)); s_ve.n += 1
            p.wait("tensor", s_ve)
            for kt in range(NKD):
                bk = "t%d" % (kt % 2)
                p.use_bank(bk)
                p.op("tensor", lambda E, b=tbank[kt % 2], kt=kt: E.transpose(
                    b[:, 0:DN], wstg_bf[0:DN, kt * 128:(kt + 1) * 128],
                    ident_bf[0:DN, 0:DN]).then_inc(s_pe.h, 1)); s_pe.n += 1
                p.wait("vector", s_pe)
                p.op("vector", lambda E, b=tbank[kt % 2], kt=kt: E.tensor_copy(
                    out=dst[:, kt, :], in_=b[:, 0:DN]).then_inc(s_ve.h, 1)); s_ve.n += 1
                p.done_bank(bk, s_ve)
        prep_bc(d_WB, WtB)
        prep_bc(d_WC, WtC)

        def put_wout(mt):
            for kt in range(NKI):
                p.op("vector", lambda E, kt=kt, mt=mt: E.tensor_copy(
                    out=WtOut[:, kt, mt * 128:(mt + 1) * 128],
                    in_=wT_out[:, kt, :]).then_inc(s_ve.h, 1)); s_ve.n += 1
        prep_wT(d_Wout, D, DI, put_wout, quant=False)

        def put_ffn(dst_hbm, nkt):
            def f(mt):
                p.wait("gpsimd", s_ve)
                dma("gpsimd", dst_hbm[:, :, mt * 128:(mt + 1) * 128],
                    wT_out[:, 0:nkt, :], s_st)
            return f
        prep_wT(d_wg, DFF, D, put_ffn(d_wgT, NKD), quant=True)
        prep_wT(d_wu, DFF, D, put_ffn(d_wuT, NKD), quant=True)
        prep_wT(d_wd, D, DFF, put_ffn(d_wdT, NFF), quant=True)

        p.barrier(s_ph)
        p.flush("p0")
        es_0.close()

        es_1 = contextlib.ExitStack()
        sb1 = SB(es_1)
        xt_f = sb1("xt_f", [128, 2, D], F32)
        xt_bf = sb1("xt_bf", [128, 2, D], BF16)
        x_fm = sb1("x_fm", [128, NKD, TT], BF16)
        xval = sb1("xval", [128, NDT, TT + 3], BF16)
        zsil = sb1("zsil", [128, NDT, TT], BF16)
        dt_bf = sb1("dt_bf", [128, NDT, TT], BF16)
        dxv = sb1("dxv", [128, NDT, TT], BF16)
        xvs = sb1("xvs", [128, NDT, TT], BF16)
        csc = sb1("csc", [128, TT], F32)
        bcs = sb1("bcs", [DN, 2, TT], BF16)
        xh_f = sb1("xh_f", [128, NKD, 3], F32)
        xh_bf = sb1("xh_bf", [128, NKD, 3], BF16)

        dma("sync", xh_f[:], d_xhalo.rearrange("(k p) t -> p k t", p=128), s_ld)
        p.wait("vector", s_ld)
        p.op("vector", lambda E: E.tensor_copy(out=xh_bf[:], in_=xh_f[:]).then_inc(s_ve.h, 1)); s_ve.n += 1

        for i in range(NT):
            p.wait("vector", s_st)
            p.wait("scalar", s_st)
            for hf in range(2):
                p.wait("sync", s_ve)
                p.wait("vector", s_pe)
                dma("sync", xt_f[:], d_x[i * TT + hf * 256:i * TT + (hf + 1) * 256, :]
                    .rearrange("(s p) d -> p s d", p=128), s_ld)
                p.wait("vector", s_ld)
                p.op("vector", lambda E: E.tensor_copy(out=xt_bf[:], in_=xt_f[:]).then_inc(s_ve.h, 1)); s_ve.n += 1
                p.wait("tensor", s_ve)
                for ft in range(NKD):
                    bk = "t%d" % (ft % 2)
                    p.use_bank(bk)
                    for st in range(2):
                        p.op("tensor", lambda E, b=tbank[ft % 2], st=st, ft=ft: E.transpose(
                            b[:, st * 128:(st + 1) * 128],
                            xt_bf[:, st, ft * 128:(ft + 1) * 128],
                            ident_bf[:]).then_inc(s_pe.h, 1)); s_pe.n += 1
                    p.wait("vector", s_pe)
                    p.op("vector", lambda E, b=tbank[ft % 2], ft=ft, hf=hf: E.tensor_copy(
                        out=x_fm[:, ft, hf * 256:(hf + 1) * 256], in_=b[:, 0:256]).then_inc(s_ve.h, 1)); s_ve.n += 1
                    p.done_bank(bk, s_ve)

            if i == 0:
                p.wait("tensor", s_ve)
                for dt_i in range(NDT):
                    bk = 4 + (dt_i % 2)
                    p.use_bank(bk)
                    def hmm(E, b=pbank[bk], dt_i=dt_i):
                        for kt in range(NKD):
                            ins = E.matmul(b[:, 0:3], WtIn[:, kt, dt_i * 128:(dt_i + 1) * 128],
                                           xh_bf[:, kt, :], start=(kt == 0), stop=(kt == NKD - 1))
                            if kt == NKD - 1:
                                ins.then_inc(s_pe.h, 1)
                    p.op("tensor", hmm); s_pe.n += 1
                    p.wait("vector", s_pe)
                    p.op("vector", lambda E, b=pbank[bk], dt_i=dt_i: E.tensor_copy(
                        out=xval[:, dt_i, 0:3], in_=b[:, 0:3]).then_inc(s_ve.h, 1)); s_ve.n += 1
                    p.done_bank(bk, s_ve)
            else:
                p.op("vector", lambda E: E.tensor_copy(
                    out=xval[:, :, 0:3], in_=xval[:, :, TT:TT + 3]).then_inc(s_ve.h, 1)); s_ve.n += 1

            for ft in range(2 * NKI):
                bk = ft % 2
                p.use_bank(bk)
                def mmxz(E, b=pbank[bk], ft=ft):
                    for kt in range(NKD):
                        ins = E.matmul(b[:], WtIn[:, kt, ft * 128:(ft + 1) * 128],
                                       x_fm[:, kt, :], start=(kt == 0), stop=(kt == NKD - 1))
                        if kt == NKD - 1:
                            ins.then_inc(s_pe.h, 1)
                p.op("tensor", mmxz); s_pe.n += 1
                if ft < NDT:
                    p.wait("vector", s_pe)
                    p.op("vector", lambda E, b=pbank[bk], ft=ft: E.tensor_copy(
                        out=xval[:, ft, 3:TT + 3], in_=b[:]).then_inc(s_ve.h, 1)); s_ve.n += 1
                    p.done_bank(bk, s_ve)
                else:
                    p.wait("scalar", s_pe)
                    p.op("scalar", lambda E, b=pbank[bk], ft=ft: E.activation(
                        out=zsil[:, ft - NDT, :], in_=b[:], func=AF.Silu).then_inc(s_ac.h, 1)); s_ac.n += 1
                    p.done_bank(bk, s_ac)

            for dt_i in range(NDT):
                bk = 2 + (dt_i % 2)
                p.use_bank(bk)
                p.op("tensor", lambda E, b=pbank[bk], dt_i=dt_i: E.matmul(
                    b[:], WtDt[:, dt_i * 128:(dt_i + 1) * 128], x_fm[0:64, 0, :],
                    start=True, stop=True).then_inc(s_pe.h, 1)); s_pe.n += 1
                p.wait("scalar", s_pe)
                p.op("scalar", lambda E, b=pbank[bk], dt_i=dt_i: E.activation(
                    out=b[:], in_=b[:], func=AF.Exp, bias=bdt[:, dt_i:dt_i + 1]).then_inc(s_ac.h, 1)); s_ac.n += 1
                p.op("scalar", lambda E, b=pbank[bk], dt_i=dt_i: E.activation(
                    out=dt_bf[:, dt_i, :], in_=b[:], func=AF.Ln, bias=1.0).then_inc(s_ac.h, 1)); s_ac.n += 1
                p.done_bank(bk, s_ac)

            for which, wt in ((0, WtB), (1, WtC)):
                bk = 4 + which
                p.use_bank(bk)
                def mmbc(E, b=pbank[bk], wt=wt):
                    for kt in range(NKD):
                        ins = E.matmul(b[0:DN, :], wt[:, kt, :], x_fm[:, kt, :],
                                       start=(kt == 0), stop=(kt == NKD - 1))
                        if kt == NKD - 1:
                            ins.then_inc(s_pe.h, 1)
                p.op("tensor", mmbc); s_pe.n += 1
                p.wait("vector", s_pe)
                p.op("vector", lambda E, b=pbank[bk], w=which: E.tensor_copy(
                    out=bcs[:, w, :], in_=b[0:DN, :]).then_inc(s_ve.h, 1)); s_ve.n += 1
                p.done_bank(bk, s_ve)

            for dt_i in range(NDT):
                p.op("vector", lambda E, d=dt_i: E.tensor_scalar(
                    out=csc[:], in0=xval[:, d, 0:TT], scalar1=cw[:, d, 0:1],
                    scalar2=None, op0=ALU.mult).then_inc(s_ve.h, 1)); s_ve.n += 1
                for k in (1, 2, 3):
                    p.op("vector", lambda E, d=dt_i, k=k: E.scalar_tensor_tensor(
                        out=csc[:], in0=xval[:, d, k:TT + k], scalar=cw[:, d, k:k + 1],
                        in1=csc[:], op0=ALU.mult, op1=ALU.add).then_inc(s_ve.h, 1)); s_ve.n += 1
                p.wait("scalar", s_ve)
                p.op("scalar", lambda E, d=dt_i: E.activation(
                    out=xvs[:, d, :], in_=csc[:], func=AF.Silu,
                    bias=cb[:, d:d + 1]).then_inc(s_ac.h, 1)); s_ac.n += 1
                p.wait("vector", s_ac)
                p.op("vector", lambda E, d=dt_i: E.tensor_tensor(
                    out=dxv[:, d, :], in0=dt_bf[:, d, :], in1=xvs[:, d, :],
                    op=ALU.mult).then_inc(s_ve.h, 1)); s_ve.n += 1

            p.wait("gpsimd", s_ve)
            p.wait("gpsimd", s_ac)
            dma("gpsimd", d_dt[i], dt_bf[:], s_st)
            dma("gpsimd", d_dx[i], dxv[:], s_st)
            dma("gpsimd", d_zs[i], zsil[:], s_st)
            dma("gpsimd", d_bc[i, 0], bcs[:, 0, :], s_st)
            dma("gpsimd", d_bc[i, 1], bcs[:, 1, :], s_st)

        rg = [[0, 1], [2, 3], [4, 5], [6, 7]]
        p.wait("gpsimd", s_st)
        if _DBG:
            dma("gpsimd", d_dbg["dt"][:], d_dt[:], s_st)
            dma("gpsimd", d_dbg["dx"][:], d_dx[:], s_st)
            dma("gpsimd", d_dbg["zs"][:], d_zs[:], s_st)
            dma("gpsimd", d_dbg["bc"][:], d_bc[:], s_st)
            dma("gpsimd", d_dbg["wgT"][:], d_wgT[:], s_st)
        for a, b in ((d_dt, d_dt_ag), (d_dx, d_dx_ag), (d_bc, d_bc_ag)):
            p.op("gpsimd", lambda E, a=a, b=b: E.collective_compute(
                "AllGather", ALU.bypass, replica_groups=rg,
                ins=[a[:]], outs=[b[:]]).then_inc(s_cc.h, 1)); s_cc.n += 1
        p.barrier(s_ph)
        p.flush("p1a")
        es_1.close()
        es_a.close()

        _ccv = s_cc.n
        for arm in nc.switch_core_id(engines=[nc.sync], n=128):
            h = (arm.logical % 8) & 1
            nc.sync.wait_ge(s_cc.h, _ccv)
            nc.sync.dma_start(
                out=d_dt_my[:],
                in_=d_dt_ag.rearrange("w t p d c -> (w t) p d c")[:, :, 8 * h:8 * h + 8, :]
            ).then_inc(s_ld.h, 16)
            nc.sync.dma_start(
                out=d_dx_my[:],
                in_=d_dx_ag.rearrange("w t p d c -> (w t) p d c")[:, :, 8 * h:8 * h + 8, :]
            ).then_inc(s_ld.h, 16)
        s_ld.n += 32
        p.wm["sync"][s_cc.name] = _ccv
        restage_tok = p.tok(s_ld)

        es_b = contextlib.ExitStack()
        sbb = SB(es_b)
        bc2 = sbb("bc2", [33, 2, DN, TT], BF16)
        Brep = sbb("Brep", [128, DN, TT], BF16)
        Crep = sbb("Crep", [128, DN, TT], BF16)
        dts = sbb("dts", [128, 2, NDH, TT], BF16)
        dxs = sbb("dxs", [128, 2, NDH, TT], BF16)
        dA_f = sbb("dA_f", [128, 4, TT], F32)
        u_all = sbb("u_all", [128, DN, TT], BF16)
        h_all = sbb("h_all", [128, DN, TT], BF16)
        hc_all = sbb("hc_all", [128, DN, TT], BF16)
        hstate = sbb("hstate", [128, NDH, DN], F32)
        y_sb = sbb("y_sb", [128, 2, TT], BF16)
        ysb_tok = [None, None]

        p.wait("gpsimd", s_cc)
        p.wait("vector", s_cc)
        p.op("vector", lambda E: E.memset(hstate[:], 0.0).then_inc(s_ve.h, 1)); s_ve.n += 1

        slice_toks = {}
        bcl_toks = {}
        scan_hist = [None] * 16
        for c in range(NCH):
            t0 = c * TT
            hh, tt_i = divmod(c, NT)
            bsl = c % 2
            tokb = bcl_toks.get(c - 2)
            if tokb is not None:
                p.wait("gpsimd", tokb[0], tokb[1])
            dma("gpsimd", bc2[0:1, bsl].rearrange("o n t -> o (n t)"),
                d_bc_ag[hh, tt_i, 0].rearrange("n t -> () (n t)"), s_bc)
            dma("gpsimd", bc2[32:33, bsl].rearrange("o n t -> o (n t)"),
                d_bc_ag[hh, tt_i, 1].rearrange("n t -> () (n t)"), s_bc)
            bc_tok = p.tok(s_bc)
            p.wait("tensor", s_bc, bc_tok[1])
            for n in range(DN):
                bk = n % 3
                p.use_bank(bk)
                p.use_bank(3 + bk)
                def bcmm(E, b=pbank[bk], b2=pbank[3 + bk], n=n, bsl=bsl):
                    E.matmul(b[:], ones_bf[0:1, :], bc2[0:1, bsl, n, :],
                             start=True, stop=True)
                    E.matmul(b2[:], ones_bf[32:33, :], bc2[32:33, bsl, n, :],
                             start=True, stop=True).then_inc(s_pe.h, 1)
                p.op("tensor", bcmm); s_pe.n += 1
                p.wait("vector", s_pe)
                p.op("vector", lambda E, b=pbank[bk], n=n: E.tensor_copy(
                    out=Brep[:, n, :], in_=b[:]).then_inc(s_ve.h, 1)); s_ve.n += 1
                p.op("vector", lambda E, b2=pbank[3 + bk], n=n: E.tensor_copy(
                    out=Crep[:, n, :], in_=b2[:]).then_inc(s_ve.h, 1)); s_ve.n += 1
                p.done_bank(bk, s_ve)
                p.done_bank(3 + bk, s_ve)
            rep_tok = p.tok(s_ve)
            bcl_toks[c] = p.tok(s_pe)
            if _DBG and c < 2:
                p.wait("gpsimd", s_ve)
                dma("gpsimd", d_dbg["brep"][c], Brep[:], s_st)
                dma("gpsimd", d_dbg["crep"][c], Crep[:], s_st)
                dma("gpsimd", d_dbg["bc2"][c], bc2[:], s_st)
                bcl_toks[c] = (s_st, s_st.n)

            cb_i = c % 2
            p.wait_tok("sync", restage_tok)
            tokp = slice_toks.get(c - 2)
            if tokp is not None:
                p.wait("sync", tokp[0], tokp[1])
            dma("sync", dts[:, cb_i, :, :], d_dt_my[c], s_ld)
            dma("sync", dxs[:, cb_i, :, :], d_dx_my[c], s_ld)
            ld_tok = p.tok(s_ld)

            for j in range(NDH):
                p.wait_tok("vector", ld_tok)
                p.wait_tok("vector", rep_tok)
                p.op("vector", lambda E, cb_i=cb_i, j=j: E.tensor_tensor(
                    out=u_all[:],
                    in0=dxs[:, cb_i, j, :].unsqueeze(1).broadcast_to([128, DN, TT]),
                    in1=Brep[:], op=ALU.mult).then_inc(s_ve.h, 1)); s_ve.n += 1
                p.wait("vector", s_gp)
                for n in range(DN):
                    p.wait("scalar", s_ld, ld_tok[1])
                    if scan_hist[(n + 12) % 16] is not None:
                        p.wait("scalar", s_ve, scan_hist[(n + 12) % 16])
                    p.op("scalar", lambda E, cb_i=cb_i, j=j, n=n: E.activation(
                        out=dA_f[:, n % 4, :], in_=dts[:, cb_i, j, :], func=AF.Exp,
                        scale=A_sb[:, j, n:n + 1]).then_inc(s_ac.h, 1)); s_ac.n += 1
                    p.wait("vector", s_ac)
                    p.op("vector", lambda E, j=j, n=n: E.tensor_tensor_scan(
                        out=h_all[:, n, :], data0=dA_f[:, n % 4, :], data1=u_all[:, n, :],
                        initial=hstate[:, j, n:n + 1], op0=ALU.mult,
                        op1=ALU.add).then_inc(s_ve.h, 1)); s_ve.n += 1
                    scan_hist[n] = s_ve.n
                p.wait("scalar", s_ve)
                p.op("scalar", lambda E, j=j: E.activation(
                    out=hstate[:, j, :], in_=h_all[:, :, TT - 1],
                    func=AF.Copy).then_inc(s_gp.h, 1)); s_gp.n += 1
                p.op("vector", lambda E: E.tensor_tensor(
                    out=hc_all[:], in0=h_all[:], in1=Crep[:], op=ALU.mult).then_inc(s_ve.h, 1)); s_ve.n += 1
                p.op("vector", lambda E: E.tensor_tensor(
                    out=hc_all[:, 0:8, :], in0=hc_all[:, 0:8, :], in1=hc_all[:, 8:16, :],
                    op=ALU.add).then_inc(s_ve.h, 1)); s_ve.n += 1
                p.op("vector", lambda E: E.tensor_tensor(
                    out=hc_all[:, 0:4, :], in0=hc_all[:, 0:4, :], in1=hc_all[:, 4:8, :],
                    op=ALU.add).then_inc(s_ve.h, 1)); s_ve.n += 1
                p.op("vector", lambda E: E.tensor_tensor(
                    out=hc_all[:, 0:2, :], in0=hc_all[:, 0:2, :], in1=hc_all[:, 2:4, :],
                    op=ALU.add).then_inc(s_ve.h, 1)); s_ve.n += 1
                jb2 = j % 2
                if ysb_tok[jb2] is not None:
                    p.wait("vector", ysb_tok[jb2][0], ysb_tok[jb2][1])
                p.op("vector", lambda E, jb2=jb2: E.tensor_tensor(
                    out=y_sb[:, jb2, :], in0=hc_all[:, 0, :], in1=hc_all[:, 1, :],
                    op=ALU.add).then_inc(s_ve.h, 1)); s_ve.n += 1
                p.wait("gpsimd", s_ve)
                dma("gpsimd", d_y[j, :, t0:t0 + TT], y_sb[:, jb2, :], s_st)
                ysb_tok[jb2] = p.tok(s_st)
            if _DBG and c < 4:
                p.wait("gpsimd", s_gp)
                dma("gpsimd", d_dbg["hst"][c], hstate[:], s_st)
                if c == 0:
                    dma("gpsimd", d_dbg["hall"][:], h_all[:], s_st)
            slice_toks[c] = (s_st, s_st.n)

        p.wait("gpsimd", s_st)
        p.op("gpsimd", lambda E: E.collective_compute(
            "AllGather", ALU.bypass, replica_groups=rg,
            ins=[d_y[:]], outs=[d_y_ag[:]]).then_inc(s_cc.h, 1)); s_cc.n += 1
        if _DBG:
            dma("gpsimd", d_dbg["y"][:], d_y[:], s_st)
        p.barrier(s_ph)
        p.flush("p1b")
        es_b.close()

        _ccv2 = s_cc.n
        for arm in nc.switch_core_id(engines=[nc.sync], n=128):
            h = (arm.logical % 8) & 1
            nc.sync.wait_ge(s_cc.h, _ccv2)
            nc.sync.dma_start(
                out=d_y_my[:],
                in_=d_y_ag[:, :, :, h * TLOC:(h + 1) * TLOC]).then_inc(s_ld.h, 16)
        s_ld.n += 16
        p.wm["sync"][s_cc.name] = _ccv2
        yres_tok = p.tok(s_ld)

        es_c = contextlib.ExitStack()
        sbc = SB(es_c)
        ysz = sbc("ysz", [128, NKI, TT], BF16)
        zs2 = sbc("zs2", [128, NKI, TT], BF16)
        ao_f = sbc("ao_f", [128, NKD, TT], F32)
        aot_f = sbc("aot_f", [128, D], F32)
        xr_f = sbc("xr_f", [128, D], F32)
        x1_f = sbc("x1_f", [128, 4, D], F32)
        sq_f = sbc("sq_f", [128, D], F32)

        def ln_norm(src_col_ap, ob_src, dst_fn):
            pass

        if _DBG:
            p.wait_tok("gpsimd", yres_tok)
            dma("gpsimd", d_dbg["ymy"][:], d_y_my[:], s_st)
        for i in range(NT):
            p.wait_tok("sync", yres_tok)
            p.wait("sync", s_ve)
            p.wait("vector", s_st)
            for jg in range(NDT):
                dma("sync", ysz[:, jg, :], d_y_my[jg // NDH, jg % NDH, :, i * TT:(i + 1) * TT], s_ld)
            dma("sync", zs2[:], d_zs[i], s_ld)
            p.wait("vector", s_ld)
            p.op("vector", lambda E: E.tensor_tensor(
                out=ysz[:], in0=ysz[:], in1=zs2[:], op=ALU.mult).then_inc(s_ve.h, 1)); s_ve.n += 1
            p.wait("tensor", s_ve)
            for ft in range(NKD):
                bk = ft % 2
                p.use_bank(bk)
                def mmo(E, b=pbank[bk], ft=ft):
                    for kt in range(NKI):
                        ins = E.matmul(b[:], WtOut[:, kt, ft * 128:(ft + 1) * 128],
                                       ysz[:, kt, :], start=(kt == 0), stop=(kt == NKI - 1))
                        if kt == NKI - 1:
                            ins.then_inc(s_pe.h, 1)
                p.op("tensor", mmo); s_pe.n += 1
                p.wait("vector", s_pe)
                p.op("vector", lambda E, b=pbank[bk], ft=ft: E.tensor_copy(
                    out=ao_f[:, ft, :], in_=b[:]).then_inc(s_ve.h, 1)); s_ve.n += 1
                p.done_bank(bk, s_ve)

            if _DBG:
                p.wait("gpsimd", s_ve)
                dma("gpsimd", d_dbg["ao"][i], ao_f[:], s_st)
                dma("gpsimd", d_dbg["ysz"][i], ysz[:], s_st)
            for st in range(4):
                p.wait("tensor", s_ve)
                p.use_bank(2)
                p.use_bank(3)
                def tx(E, st=st):
                    for ft in range(NKD):
                        b = pbank[2] if ft < 4 else pbank[3]
                        col = (ft % 4) * 128
                        ins = E.transpose(b[:, col:col + 128],
                                          ao_f[:, ft, st * 128:(st + 1) * 128], ident_f[:])
                        if ft == NKD - 1:
                            ins.then_inc(s_pe.h, 1)
                p.op("tensor", tx); s_pe.n += 1
                p.wait("sync", s_ve)
                dma("sync", xr_f[:], d_x[i * TT + st * 128:i * TT + (st + 1) * 128, :], s_ld)
                p.wait("vector", s_pe)
                p.wait("vector", s_ld)
                p.op("vector", lambda E: E.tensor_copy(
                    out=aot_f[:, 0:512], in_=pbank[2][:]).then_inc(s_ve.h, 1)); s_ve.n += 1
                p.op("vector", lambda E: E.tensor_copy(
                    out=aot_f[:, 512:1024], in_=pbank[3][:]).then_inc(s_ve.h, 1)); s_ve.n += 1
                p.done_bank(2, s_ve)
                p.done_bank(3, s_ve)
                p.op("vector", lambda E: E.tensor_tensor(
                    out=xr_f[:], in0=xr_f[:], in1=aot_f[:], op=ALU.add).then_inc(s_ve.h, 1)); s_ve.n += 1
                p.op("vector", lambda E: E.reduce_sum(
                    out=ln_s[:, 0:1], in_=xr_f[:], axis=AX.X).then_inc(s_ve.h, 1)); s_ve.n += 1
                p.wait("scalar", s_ve)
                p.op("scalar", lambda E: E.activation(
                    out=sq_f[:], in_=xr_f[:], func=AF.Square).then_inc(s_ac.h, 1)); s_ac.n += 1
                p.wait("vector", s_ac)
                p.op("vector", lambda E: E.reduce_sum(
                    out=ln_s[:, 1:2], in_=sq_f[:], axis=AX.X).then_inc(s_ve.h, 1)); s_ve.n += 1
                p.wait("gpsimd", s_ve)
                p.op("gpsimd", lambda E: E.tensor_scalar(
                    out=ln_s[:, 2:3], in0=ln_s[:, 0:1], scalar1=1.0 / D, scalar2=None,
                    op0=ALU.mult)); 
                p.op("gpsimd", lambda E: E.tensor_tensor(
                    out=ln_s[:, 3:4], in0=ln_s[:, 2:3], in1=ln_s[:, 2:3], op=ALU.mult))
                p.op("gpsimd", lambda E: E.tensor_scalar(
                    out=ln_s[:, 4:5], in0=ln_s[:, 1:2], scalar1=1.0 / D, scalar2=None,
                    op0=ALU.mult))
                p.op("gpsimd", lambda E: E.tensor_tensor(
                    out=ln_s[:, 5:6], in0=ln_s[:, 4:5], in1=ln_s[:, 3:4], op=ALU.subtract))
                p.op("gpsimd", lambda E: E.tensor_scalar(
                    out=ln_s[:, 5:6], in0=ln_s[:, 5:6], scalar1=1e-5, scalar2=None,
                    op0=ALU.add).then_inc(s_gp.h, 1)); s_gp.n += 1
                p.wait("scalar", s_gp)
                p.op("scalar", lambda E: E.activation(
                    out=ln_s[:, 6:7], in_=ln_s[:, 5:6], func=AF.Ln).then_inc(s_ac.h, 1)); s_ac.n += 1
                p.op("scalar", lambda E: E.mul(out=ln_s[:, 6:7], in_=ln_s[:, 6:7], mul=-0.5).then_inc(s_ac.h, 1)); s_ac.n += 1
                p.op("scalar", lambda E: E.activation(
                    out=ln_s[:, 7:8], in_=ln_s[:, 6:7], func=AF.Exp).then_inc(s_ac.h, 1)); s_ac.n += 1
                p.wait("vector", s_ac)
                p.op("vector", lambda E, st=st: E.tensor_scalar(
                    out=x1_f[:, st, :], in0=xr_f[:], scalar1=ln_s[:, 2:3],
                    scalar2=ln_s[:, 7:8], op0=ALU.subtract, op1=ALU.mult).then_inc(s_ve.h, 1)); s_ve.n += 1
                if _DBG:
                    p.wait("gpsimd", s_ve)
                    dma("gpsimd", d_dbg["xr"][i * 4 + st], xr_f[:], s_st)
                    dma("gpsimd", d_dbg["lns"][i * 4 + st], ln_s[:], s_st)
            p.wait("gpsimd", s_ve)
            for st in range(4):
                dma("gpsimd", d_x1[i * 4 + st], x1_f[:, st, :], s_st)
        if _DBG:
            p.wait("gpsimd", s_st)
            dma("gpsimd", d_dbg["x1"][:], d_x1[:], s_st)
        p.barrier(s_ph)
        p.flush("p1c")
        es_c.close()
        es_w.close()

        es_2 = contextlib.ExitStack()
        sb2 = SB(es_2)
        WtG = sb2("WtG", [128, NKD, DFF], BF16)
        WtU_s = sb2("WtU_s", [128, 2, NKD, 128], BF16)
        WtD_s = sb2("WtD_s", [128, 2, NFF, 128], BF16)
        x1t = sb2("x1t", [128, 2, D], F32)
        x1r = sb2("x1r", [128, 2, D], F32)
        xq_tm = sb2("xq_tm", [128, 4, D], BF16)
        xq_fm = sb2("xq_fm", [128, NKD, TT], BF16)
        sg_t = sb2("sg_t", [128, TT], BF16)
        h_fm = sb2("h_fm", [128, NFF, TT], BF16)
        h_tm = sb2("h_tm", [128, DFF], BF16)
        do_f = sb2("do_f", [128, NKD, TT], F32)
        o_f = sb2("o_f", [128, 2, D], F32)
        sq_f = sb2("sq_f2", [128, D], F32)
        of_tok = [None, None]
        quant2 = mk_quant(sb2, "q2", DFF, s_ve, s_ac)

        p.wait("sync", s_st)
        dma("sync", WtG[:], d_wgT[:], s_ld)
        wld_tok = p.tok(s_ld)

        for i in range(NT):
            for st in range(4):
                stb = st % 2
                p.wait("sync", s_ve)
                p.wait("sync", s_ac)
                dma("sync", x1t[:, stb, :], d_x1[i * 4 + st], s_ld)
                p.wait("vector", s_ld)
                p.wait("scalar", s_ld)
                quant2(x1t[:, stb, :], D, xq_tm[:, st, :])
            p.wait("tensor", s_ve)
            for ft in range(NKD):
                bk = "t%d" % (ft % 2)
                p.use_bank(bk)
                def txq(E, ft=ft):
                    for st in range(4):
                        ins = E.transpose(tbank[ft % 2][:, st * 128:(st + 1) * 128],
                                          xq_tm[:, st, ft * 128:(ft + 1) * 128], ident_bf[:])
                        if st == 3:
                            ins.then_inc(s_pe.h, 1)
                p.op("tensor", txq); s_pe.n += 1
                p.wait("vector", s_pe)
                p.op("vector", lambda E, b=ft % 2, ft=ft: E.tensor_copy(
                    out=xq_fm[:, ft, :], in_=tbank[b][:, 0:512]).then_inc(s_ve.h, 1)); s_ve.n += 1
                p.done_bank(bk, s_ve)
            xq_tok = p.tok(s_ve)

            p.wait_tok("tensor", xq_tok)
            p.wait_tok("tensor", wld_tok)
            wu_ld = {}
            wu_mmtok = {}
            for ft in range(NFF + 1):
                if ft < NFF:
                    wub_n = ft % 2
                    if ft - 2 in wu_mmtok:
                        p.wait("sync", wu_mmtok[ft - 2][0], wu_mmtok[ft - 2][1])
                    else:
                        p.wait("sync", s_pe)
                    dma("sync", WtU_s[:, wub_n, :, :], d_wuT[:, :, ft * 128:(ft + 1) * 128], s_ld)
                    wu_ld[ft] = p.tok(s_ld)
                if ft == 0:
                    continue
                ft = ft - 1
                bkg, bku = 2 + (ft % 2), 4 + (ft % 2)
                wub = ft % 2
                p.wait("tensor", wu_ld[ft][0], wu_ld[ft][1])
                p.use_bank(bkg)
                p.use_bank(bku)
                def mmgu(E, ft=ft, bkg=bkg, bku=bku, wub=wub):
                    for kt in range(NKD):
                        E.matmul(pbank[bkg][:], WtG[:, kt, ft * 128:(ft + 1) * 128],
                                 xq_fm[:, kt, :], start=(kt == 0), stop=(kt == NKD - 1))
                    for kt in range(NKD):
                        ins = E.matmul(pbank[bku][:], WtU_s[:, wub, kt, :],
                                       xq_fm[:, kt, :], start=(kt == 0), stop=(kt == NKD - 1))
                        if kt == NKD - 1:
                            ins.then_inc(s_pe.h, 1)
                p.op("tensor", mmgu); s_pe.n += 1
                wu_mmtok[ft] = p.tok(s_pe)
                p.wait("scalar", s_pe)
                p.op("scalar", lambda E, bkg=bkg: E.activation(
                    out=sg_t[:], in_=pbank[bkg][:], func=AF.Silu).then_inc(s_ac.h, 1)); s_ac.n += 1
                p.done_bank(bkg, s_ac)
                p.wait("vector", s_ac)
                p.op("vector", lambda E, bku=bku, ft=ft: E.tensor_tensor(
                    out=h_fm[:, ft, :], in0=sg_t[:], in1=pbank[bku][:],
                    op=ALU.mult).then_inc(s_ve.h, 1)); s_ve.n += 1
                p.done_bank(bku, s_ve)
            h_tok = p.tok(s_ve)

            for st in range(4):
                p.wait_tok("tensor", h_tok)
                for g in range(3):
                    bk = "t%d" % (g % 2)
                    p.use_bank(bk)
                    kts = list(range(g * 8, min(g * 8 + 8, NFF)))
                    def txh(E, st=st, kts=kts, g=g):
                        for kk, kt in enumerate(kts):
                            ins = E.transpose(tbank[g % 2][:, kk * 128:kk * 128 + 128],
                                              h_fm[:, kt, st * 128:(st + 1) * 128], ident_bf[:])
                            if kk == len(kts) - 1:
                                ins.then_inc(s_pe.h, 1)
                    p.op("tensor", txh); s_pe.n += 1
                    p.wait("vector", s_pe)
                    for kk, kt in enumerate(kts):
                        p.op("vector", lambda E, g=g, kk=kk, kt=kt: E.tensor_copy(
                            out=h_tm[:, kt * 128:(kt + 1) * 128],
                            in_=tbank[g % 2][:, kk * 128:kk * 128 + 128]).then_inc(s_ve.h, 1)); s_ve.n += 1
                    p.done_bank(bk, s_ve)
                p.wait("scalar", s_ve)
                quant2(h_tm[:], DFF, h_tm[:])
                p.wait("tensor", s_ve)
                for g in range(3):
                    bk = "t%d" % (g % 2)
                    p.use_bank(bk)
                    kts = list(range(g * 8, min(g * 8 + 8, NFF)))
                    def txhq(E, kts=kts, g=g):
                        for kk, kt in enumerate(kts):
                            ins = E.transpose(tbank[g % 2][:, kk * 128:kk * 128 + 128],
                                              h_tm[:, kt * 128:(kt + 1) * 128], ident_bf[:])
                            if kk == len(kts) - 1:
                                ins.then_inc(s_pe.h, 1)
                    p.op("tensor", txhq); s_pe.n += 1
                    p.wait("vector", s_pe)
                    p.op("vector", lambda E, g=g, kts=kts, st=st: E.tensor_copy(
                        out=h_fm[:, kts[0]:kts[0] + len(kts), st * 128:(st + 1) * 128],
                        in_=tbank[g % 2][:, 0:len(kts) * 128].rearrange(
                            "p (k c) -> p k c", c=128)).then_inc(s_ve.h, 1)); s_ve.n += 1
                    p.done_bank(bk, s_ve)
            hqf_tok = p.tok(s_ve)

            p.wait_tok("tensor", hqf_tok)
            wd_ld = {}
            wd_mmtok = {}
            for ft in range(NKD + 1):
                if ft < NKD:
                    wb_n = ft % 2
                    if ft - 2 in wd_mmtok:
                        p.wait("sync", wd_mmtok[ft - 2][0], wd_mmtok[ft - 2][1])
                    else:
                        p.wait("sync", s_pe)
                    dma("sync", WtD_s[:, wb_n, :, :], d_wdT[:, :, ft * 128:(ft + 1) * 128], s_ld)
                    wd_ld[ft] = p.tok(s_ld)
                if ft == 0:
                    continue
                ft = ft - 1
                wb = ft % 2
                p.wait("tensor", wd_ld[ft][0], wd_ld[ft][1])
                bk = 2 + (ft % 2)
                p.use_bank(bk)
                def mmd(E, ft=ft, bk=bk, wb=wb):
                    for kt in range(NFF):
                        ins = E.matmul(pbank[bk][:], WtD_s[:, wb, kt, :],
                                       h_fm[:, kt, :], start=(kt == 0), stop=(kt == NFF - 1))
                        if kt == NFF - 1:
                            ins.then_inc(s_pe.h, 1)
                p.op("tensor", mmd); s_pe.n += 1
                wd_mmtok[ft] = p.tok(s_pe)
                p.wait("vector", s_pe)
                p.op("vector", lambda E, bk=bk, ft=ft: E.tensor_copy(
                    out=do_f[:, ft, :], in_=pbank[bk][:]).then_inc(s_ve.h, 1)); s_ve.n += 1
                p.done_bank(bk, s_ve)

            for st in range(4):
                stb = st % 2
                p.wait("sync", s_ve)
                dma("sync", x1r[:, stb, :], d_x1[i * 4 + st], s_ld)
                p.wait("tensor", s_ve)
                p.use_bank(4)
                p.use_bank(5)
                def txd(E, st=st):
                    for ft in range(NKD):
                        b = pbank[4] if ft < 4 else pbank[5]
                        col = (ft % 4) * 128
                        ins = E.transpose(b[:, col:col + 128],
                                          do_f[:, ft, st * 128:(st + 1) * 128], ident_f[:])
                        if ft == NKD - 1:
                            ins.then_inc(s_pe.h, 1)
                p.op("tensor", txd); s_pe.n += 1
                p.wait("vector", s_pe)
                ob = st % 2
                if of_tok[ob] is not None:
                    p.wait("vector", of_tok[ob][0], of_tok[ob][1])
                p.op("vector", lambda E, ob=ob: E.tensor_copy(
                    out=o_f[:, ob, 0:512], in_=pbank[4][:]).then_inc(s_ve.h, 1)); s_ve.n += 1
                p.op("vector", lambda E, ob=ob: E.tensor_copy(
                    out=o_f[:, ob, 512:1024], in_=pbank[5][:]).then_inc(s_ve.h, 1)); s_ve.n += 1
                p.done_bank(4, s_ve)
                p.done_bank(5, s_ve)
                p.wait("vector", s_ld)
                p.op("vector", lambda E, stb=stb, ob=ob: E.tensor_tensor(
                    out=o_f[:, ob, :], in0=o_f[:, ob, :], in1=x1r[:, stb, :],
                    op=ALU.add).then_inc(s_ve.h, 1)); s_ve.n += 1
                p.op("vector", lambda E, ob=ob: E.reduce_sum(
                    out=ln_s[:, 0:1], in_=o_f[:, ob, :], axis=AX.X).then_inc(s_ve.h, 1)); s_ve.n += 1
                p.wait("scalar", s_ve)
                p.op("scalar", lambda E, ob=ob: E.activation(
                    out=sq_f[:], in_=o_f[:, ob, :], func=AF.Square).then_inc(s_ac.h, 1)); s_ac.n += 1
                p.wait("vector", s_ac)
                p.op("vector", lambda E: E.reduce_sum(
                    out=ln_s[:, 1:2], in_=sq_f[:], axis=AX.X).then_inc(s_ve.h, 1)); s_ve.n += 1
                p.wait("gpsimd", s_ve)
                p.op("gpsimd", lambda E: E.tensor_scalar(
                    out=ln_s[:, 2:3], in0=ln_s[:, 0:1], scalar1=1.0 / D, scalar2=None,
                    op0=ALU.mult)); 
                p.op("gpsimd", lambda E: E.tensor_tensor(
                    out=ln_s[:, 3:4], in0=ln_s[:, 2:3], in1=ln_s[:, 2:3], op=ALU.mult))
                p.op("gpsimd", lambda E: E.tensor_scalar(
                    out=ln_s[:, 4:5], in0=ln_s[:, 1:2], scalar1=1.0 / D, scalar2=None,
                    op0=ALU.mult))
                p.op("gpsimd", lambda E: E.tensor_tensor(
                    out=ln_s[:, 5:6], in0=ln_s[:, 4:5], in1=ln_s[:, 3:4], op=ALU.subtract))
                p.op("gpsimd", lambda E: E.tensor_scalar(
                    out=ln_s[:, 5:6], in0=ln_s[:, 5:6], scalar1=1e-5, scalar2=None,
                    op0=ALU.add).then_inc(s_gp.h, 1)); s_gp.n += 1
                p.wait("scalar", s_gp)
                p.op("scalar", lambda E: E.activation(
                    out=ln_s[:, 6:7], in_=ln_s[:, 5:6], func=AF.Ln).then_inc(s_ac.h, 1)); s_ac.n += 1
                p.op("scalar", lambda E: E.mul(out=ln_s[:, 6:7], in_=ln_s[:, 6:7], mul=-0.5).then_inc(s_ac.h, 1)); s_ac.n += 1
                p.op("scalar", lambda E: E.activation(
                    out=ln_s[:, 7:8], in_=ln_s[:, 6:7], func=AF.Exp).then_inc(s_ac.h, 1)); s_ac.n += 1
                p.wait("vector", s_ac)
                p.op("vector", lambda E, ob=ob: E.tensor_scalar(
                    out=o_f[:, ob, :], in0=o_f[:, ob, :], scalar1=ln_s[:, 2:3],
                    scalar2=ln_s[:, 7:8], op0=ALU.subtract, op1=ALU.mult).then_inc(s_ve.h, 1)); s_ve.n += 1
                p.wait("gpsimd", s_ve)
                dma("gpsimd", d_out[i * TT + st * 128:i * TT + (st + 1) * 128, :],
                    o_f[:, ob, :], s_st)
                of_tok[ob] = p.tok(s_st)

        p.wait("gpsimd", s_st)
        p.flush("p2")
        es_2.close()
    return nc


_NC_CACHE = []


def kernel(L0_attn_x, L0_attn_W_in, L0_attn_conv_w, L0_attn_conv_b, L0_attn_A_log,
           L0_attn_W_dt, L0_attn_b_dt, L0_attn_W_B, L0_attn_W_C, L0_attn_W_out,
           L0_ffn_w_gate, L0_ffn_w_up, L0_ffn_w_down):
    x = np.ascontiguousarray(np.asarray(L0_attn_x, dtype=np.float32))
    W_in = np.ascontiguousarray(L0_attn_W_in, dtype=np.float32)
    conv_w = np.ascontiguousarray(np.asarray(L0_attn_conv_w, dtype=np.float32).reshape(DI, 4))
    conv_b = np.ascontiguousarray(np.asarray(L0_attn_conv_b, dtype=np.float32).reshape(1, DI))
    A_log = np.ascontiguousarray(L0_attn_A_log, dtype=np.float32)
    W_dt = np.ascontiguousarray(L0_attn_W_dt, dtype=np.float32)
    b_dt = np.ascontiguousarray(np.asarray(L0_attn_b_dt, dtype=np.float32).reshape(1, DI))
    W_B = np.ascontiguousarray(L0_attn_W_B, dtype=np.float32)
    W_C = np.ascontiguousarray(L0_attn_W_C, dtype=np.float32)
    W_out = np.ascontiguousarray(L0_attn_W_out, dtype=np.float32)
    w_gate = np.ascontiguousarray(L0_ffn_w_gate, dtype=np.float32)
    w_up = np.ascontiguousarray(L0_ffn_w_up, dtype=np.float32)
    w_down = np.ascontiguousarray(L0_ffn_w_down, dtype=np.float32)
    ident = np.eye(128, dtype=np.float32)

    if not _NC_CACHE:
        _NC_CACHE.append(build_program())
    nc = _NC_CACHE[0]

    in_maps = []
    for r in range(8):
        s, h = r >> 1, r & 1
        halo = (np.zeros((D, 3), np.float32) if h == 0
                else np.ascontiguousarray(x[s, TLOC - 3:TLOC, :].T))
        in_maps.append({
            "x_loc": np.ascontiguousarray(x[s, h * TLOC:(h + 1) * TLOC, :]),
            "x_halo": halo,
            "W_in": W_in,
            "conv_w": conv_w,
            "conv_b": conv_b,
            "A_log_dh": np.ascontiguousarray(A_log[h * (DI // 2):(h + 1) * (DI // 2), :]),
            "W_dt": W_dt,
            "b_dt": b_dt,
            "W_B": W_B,
            "W_C": W_C,
            "W_out": W_out,
            "w_gate": w_gate,
            "w_up": w_up,
            "w_down": w_down,
            "ident": ident,
            "csts": np.tile(np.array([[1e-5, -0.5]], dtype=np.float32), (128, 1)),
        })
    res = run_bass_kernel_spmd(nc, in_maps, core_ids=list(range(8)))
    out = np.empty((4, SEQ, D), np.float32)
    for r in range(8):
        s, h = r >> 1, r & 1
        out[s, h * TLOC:(h + 1) * TLOC, :] = res.results[r]["out_loc"]
    return out
